# version 12
# baseline (speedup 1.0000x reference)
import numpy as np
import concourse.bass as bass
from concourse.bass import IndirectOffsetOnAxis
import concourse.mybir as mybir
from concourse.bass_utils import run_bass_kernel_spmd

F32 = mybir.dt.float32
BF16 = mybir.dt.bfloat16
AF = mybir.ActivationFunctionType
ALU = mybir.AluOpType
AX = mybir.AxisListType

P = 128
D = 1024
I = 2048
NG = 8
KCV = 4
MMLP = 4096
NE = 8
DC = D // P
IC = I // P
MC = MMLP // P
EPS = 1e-5
CAP = 1024

DBG = False


class _Sem:
    __slots__ = ("h", "n")

    def __init__(self, h):
        self.h = h
        self.n = 0


class _Buf:
    __slots__ = ("w", "r")

    def __init__(self):
        self.w = None
        self.r = []

    def rd(self):
        return [self.w] if self.w is not None else []

    def war(self):
        out = list(self.r)
        if self.w is not None:
            out.append(self.w)
        return out

    def wrote(self, tok):
        self.w = tok
        self.r = []


class _B:
    def __init__(self, nc, es):
        self.nc = nc
        self.es = es
        self.items = []
        self.waited = {}

    def sem(self, name):
        return _Sem(self.es.enter_context(self.nc.semaphore(name)))

    def op(self, eng, make, waits=(), inc=None, inck=1):
        best = {}
        for s, v in waits:
            if s is None or v <= 0:
                continue
            assert v <= s.n, f"wait {v} > built count {s.n}"
            k = id(s)
            if k not in best or best[k][1] < v:
                best[k] = (s, v)
        wl = []
        for s, v in best.values():
            key = (eng, id(s))
            if self.waited.get(key, 0) >= v:
                continue
            self.waited[key] = v
            wl.append((s.h, v))
        tok = None
        if inc is not None:
            inc.n += inck
            tok = (inc, inc.n)

        def fn(e, make=make, wl=wl, inc=inc, inck=inck):
            for h, v in wl:
                e.wait_ge(h, v)
            ins = make(e)
            if inc is not None:
                ins.then_inc(inc.h, inck)

        self.items.append((eng, fn))
        return tok

    def replay(self, name, e):
        for eng, fn in self.items:
            if eng == name:
                fn(e)


def build_program(t_core=4096, stages=(1, 2, 3), dbg=False):
    T1 = 256
    T3 = 512
    NT1 = t_core // T1
    NT3 = t_core // T3
    nc = bass.Bass("TRN2", target_bir_lowering=False, debug=False)

    x_d = nc.dram_tensor("x", [t_core, D], F32, kind="ExternalInput").ap()
    xh_d = nc.dram_tensor("xh", [3, D], F32, kind="ExternalInput").ap()
    winT_d = nc.dram_tensor("winT", [D, 2 * I], F32, kind="ExternalInput").ap()
    convw_d = nc.dram_tensor("convw", [NG, KCV, 256, 256], F32, kind="ExternalInput").ap()
    convb_d = nc.dram_tensor("convb", [P, IC], F32, kind="ExternalInput").ap()
    lng_d = nc.dram_tensor("lng", [P, IC], F32, kind="ExternalInput").ap()
    lnb_d = nc.dram_tensor("lnb", [P, IC], F32, kind="ExternalInput").ap()
    fcwT_d = nc.dram_tensor("fcwT", [I, D], F32, kind="ExternalInput").ap()
    fcb_d = nc.dram_tensor("fcb", [P, DC], F32, kind="ExternalInput").ap()
    cprojwT_d = nc.dram_tensor("cprojwT", [D, I], F32, kind="ExternalInput").ap()
    cprojb_d = nc.dram_tensor("cprojb", [P, IC], F32, kind="ExternalInput").ap()
    outwT_d = nc.dram_tensor("outwT", [I, D], F32, kind="ExternalInput").ap()
    wg_d = nc.dram_tensor("wg", [P, DC, NE], F32, kind="ExternalInput").ap()
    bg_d = nc.dram_tensor("bg", [1, NE], F32, kind="ExternalInput").ap()
    w1_d = nc.dram_tensor("w1", [NE, D, MMLP], F32, kind="ExternalInput").ap()
    b1t_d = nc.dram_tensor("b1t", [P, NE, MC], F32, kind="ExternalInput").ap()
    w2_d = nc.dram_tensor("w2", [NE, MMLP, D], F32, kind="ExternalInput").ap()
    b2_d = nc.dram_tensor("b2", [NE, D], F32, kind="ExternalInput").ap()
    out_d = nc.dram_tensor("out", [t_core, D], F32, kind="ExternalOutput").ap()

    NSLOT = NE * CAP
    sk = "ExternalOutput" if dbg else "Internal"
    u_sp = nc.dram_tensor("u_sp", [I, t_core], F32, kind=sk).ap()
    x1f_sp = nc.dram_tensor("x1f_sp", [t_core, D], F32, kind=sk).ap()
    gathx_sp = nc.dram_tensor("gathx_sp", [NSLOT, D], F32, kind="Internal").ap()
    ygath_sp = nc.dram_tensor("ygath_sp", [NSLOT, D], F32, kind=sk).ap()

    import contextlib
    es = contextlib.ExitStack()
    with es:
        b = _B(nc, es)

        def sb(name, shape, dt):
            return es.enter_context(nc.sbuf_tensor(name, shape, dt))

        ident = sb("ident", [P, P], F32)
        ones_cb = sb("ones_cb", [P, 1], BF16)
        ones_rb = sb("ones_rb", [1, P], BF16)
        ones_rf = sb("ones_rf", [1, P], F32)
        convb_s = sb("convb_s", [P, IC], F32)
        lng_s = sb("lng_s", [P, IC], F32)
        lnb_s = sb("lnb_s", [P, IC], F32)
        fcb_s = sb("fcb_s", [P, DC], F32)
        cprojb_s = sb("cprojb_s", [P, IC], F32)
        b1t_s = sb("b1t_s", [P, NE * MC], F32)
        wg_s = sb("wg_s", [P, DC * NE], F32)
        bg_s = sb("bg_s", [1, NE], F32)
        b2r_s = sb("b2r_s", [1, D], BF16)
        NCH = t_core // P
        mask_s = sb("mask_s", [P, NE * NCH], F32)
        lnsm_s = sb("lnsm_s", [P, 48], F32)
        rows_s = sb("rows_s", [P, NE], F32)
        poff_s = sb("poff_s", [P, NE], F32)
        tmpc_s = sb("tmpc_s", [P, NCH], F32)
        dest_s = sb("dest_s", [P, NCH], F32)
        dest_i = sb("dest_i", [P, NCH], mybir.dt.int32)
        p_tip = sb("p_tip", [P, NCH], F32)
        sm_s = sb("sm_s", [P, 32], F32)

        ov0 = (nc.sbuf_base + 31) // 32 * 32
        lim = nc.sbuf_top - ov0
        KB = 1024

        def at(name, shape, dt, off):
            esz = 2 if dt == BF16 else 4
            need = off + shape[1] * esz
            assert need <= lim, f"{name}: {need} > {lim}"
            return nc.alloc_sbuf_tensor_at(name, shape, dt, offset=ov0 + off)

        w1_s = at("w1_s", [P, 4 * DC * (MMLP // 4)], BF16, 0)
        w2_s = at("w2_s", [P, MC * D], BF16, 64 * KB)
        h3_s = at("h3_s", [P, MC * T3], BF16, 128 * KB)
        gxf_s = at("gxf_s", [P, 4 * D], F32, 160 * KB)
        gxT_s = at("gxT_s", [P, DC * T3], BF16, 176 * KB)
        xrow_s = [at(f"xrow_{i}", [P, D], F32, 176 * KB + i * 4 * KB)
                  for i in range(2)]
        ysb_s = [at(f"ysb_{i}", [P, 512], F32, 184 * KB + i * 2 * KB)
                 for i in range(2)]
        sq3_s = at("sq3_s", [P, D], F32, 188 * KB)
        b2b_s = at("b2b_s", [P, D], BF16, 192 * KB)
        oh_s = at("oh_s", [P, NE * NCH], F32, 128 * KB)
        scn_s = at("scn_s", [P, NE * NCH], F32, 129 * KB)
        zeros_s = at("zeros_s", [P, P], F32, 130 * KB)
        rT_s = at("rT_s", [NE, P], F32, 131 * KB)
        rTi_s = at("rTi_s", [NE, P], F32, 131 * KB + 512)
        yt_s = [at(f"yt_{i}", [P, 4 * D], F32, i * 16 * KB) for i in range(2)]
        x1ft_s = [at(f"x1ft_{i}", [P, 4 * D], F32, 32 * KB + i * 16 * KB)
                  for i in range(2)]

        def lay2(pairs, base):
            out, off = {}, base
            for name, nbytes in pairs:
                out[name] = off
                off += (nbytes + 31) // 32 * 32
            assert off <= lim, f"stage2 overflow {off} > {lim}"
            return out

        o2 = lay2([
            ("fcw", IC * D * 2), ("cprojw", DC * I * 2), ("outw", IC * D * 2),
            ("u2", IC * T1 * 4), ("ub", IC * T1 * 2), ("t1", IC * T1 * 4),
            ("hn", IC * T1 * 2), ("mid", DC * T1 * 2), ("sqh2", IC * T1 * 2),
            ("x2", 2 * D * 4), ("s", 2 * D * 4), ("x1f2", 2 * D * 4),
            ("stats", 4 * T1 * 4),
        ], 16 * KB)
        fcw_s = at("fcw_s", [P, IC * D], BF16, o2["fcw"])
        cprojw_s = at("cprojw_s", [P, DC * I], BF16, o2["cprojw"])
        outw_s = at("outw_s", [P, IC * D], BF16, o2["outw"])
        u2_s = at("u2_s", [P, IC * T1], F32, o2["u2"])
        ub_s = at("ub_s", [P, IC * T1], BF16, o2["ub"])
        t1_s = at("t1_s", [P, IC * T1], F32, o2["t1"])
        sqt2_s = at("sqt2_s", [P, D], F32, o2["t1"])
        hn_s = at("hn_s", [P, IC * T1], BF16, o2["hn"])
        mid_s = at("mid_s", [P, DC * T1], BF16, o2["mid"])
        sq2_s = at("sq2_s", [P, IC * T1], BF16, o2["sqh2"])
        h2_s = at("h2_s", [P, IC * T1], BF16, o2["sqh2"])
        x2_s = at("x2_s", [P, 2 * D], F32, o2["x2"])
        s_s = at("s_s", [P, 2 * D], F32, o2["s"])
        x1f2_s = at("x1f2_s", [P, 2 * D], F32, o2["x1f2"])
        x1cs_s = at("x1cs_s", [P, T1], F32, o2["stats"])
        stats_s = at("stats_s", [1, 4 * T1], F32, o2["stats"])
        x1b2_s = at("x1b2_s", [P, DC * T1], BF16, o2["mid"])

        o1 = lay2([
            ("x0", 2 * D * 4), ("x1", 2 * D * 4),
            ("fcw_hole", IC * D * 2),
            ("win", DC * 2 * I * 2), ("convw", 64 * 256 * 2),
            ("hbuf", IC * (T1 + 3) * 2), ("gate", IC * T1 * 4),
            ("u1", IC * T1 * 4), ("xT", DC * T1 * 2),
            ("xh", D * 4), ("xhT", DC * 3 * 2),
        ], 0)
        assert o1["fcw_hole"] == 16 * KB and o1["win"] == o2["cprojw"]
        x_s = [at(f"x_s{i}", [P, 2 * D], F32, o1[f"x{i}"]) for i in range(2)]
        win_s = at("win_s", [P, DC * 2 * I], BF16, o1["win"])
        convw_s = at("convw_s", [P, 64 * 256], BF16, o1["convw"])
        hbuf_s = at("hbuf_s", [P, IC * (T1 + 3)], BF16, o1["hbuf"])
        gate_s = at("gate_s", [P, IC * T1], F32, o1["gate"])
        u1_s = at("u1_s", [P, IC * T1], F32, o1["u1"])
        xT_s = at("xT_s", [P, DC * T1], BF16, o1["xT"])
        xh_s = at("xh_s", [P, D], F32, o1["xh"])
        xhT_s = at("xhT_s", [P, DC * 3], BF16, o1["xhT"])

        pb = [es.enter_context(nc.psum_tensor(f"pb{i}", [P, 512], F32))
              for i in range(8)]

        s_pe = b.sem("s_pe")
        s_vec = b.sem("s_vec")
        s_act = b.sem("s_act")
        s_gp = b.sem("s_gp")
        d_wA = b.sem("d_wA")
        d_wB = b.sem("d_wB")
        d_wC = b.sem("d_wC")
        d_wD = b.sem("d_wD")
        d_w1q = [b.sem(f"d_w1q{i}") for i in range(4)]
        d_w2q = [b.sem(f"d_w2q{i}") for i in range(4)]
        d_wb = b.sem("d_wb")
        d_sm = b.sem("d_sm")
        d_x = [b.sem("d_x0"), b.sem("d_x1")]
        d_xh = b.sem("d_xh")
        d_uo = b.sem("d_uo")
        d_ui = b.sem("d_ui")
        d_x2 = b.sem("d_x2")
        d_x1fo = b.sem("d_x1fo")
        d_xr = [b.sem("d_xr0"), b.sem("d_xr1")]
        d_sc = [b.sem("d_sc0"), b.sem("d_sc1")]
        d_gx = b.sem("d_gx")
        d_yo = [b.sem("d_yo0"), b.sem("d_yo1")]
        d_yg = b.sem("d_yg")
        d_xf = [b.sem("d_xf0"), b.sem("d_xf1")]
        d_out = [b.sem("d_out0"), b.sem("d_out1")]

        BNAMES = ["ident", "ones", "biases", "xh", "xhT", "win", "convw",
                  "fcw", "cprojw", "outw", "hbuf", "gate", "u1", "xT",
                  "x0", "x1", "u2", "ub", "t1", "hn", "mid", "sqh2", "x2",
                  "s", "x1f2", "x1cs", "w1", "w2", "h3", "gxf", "gxT",
                  "ysb_0", "ysb_1", "xrow_0", "xrow_1", "yt_0", "yt_1",
                  "x1ft_0", "x1ft_1", "sq3", "mask", "stats", "lnsm", "sm",
                  "b2r", "route", "zeros"]
        B = {n: _Buf() for n in BNAMES}
        B["x1cs"] = B["stats"]
        PB = [_Buf() for _ in range(8)]
        SP = {"u": _Buf(), "x1f": _Buf(), "x1b": _Buf(), "acc": _Buf()}

        def dma(q, out_ap, in_ap, waits, sem):
            return b.op(q, lambda e, o=out_ap, i=in_ap: e.dma_start(o, i),
                        waits=waits, inc=sem, inck=16)

        t = b.op("gpsimd", lambda e: e.memset(ident.ap(), 0.0), inc=s_gp)
        t = b.op("gpsimd", lambda e: e.affine_select(
            out=ident.ap(), in_=ident.ap(), compare_op=ALU.not_equal,
            fill=1.0, base=0, pattern=[[-1, P]], channel_multiplier=1),
            waits=[t], inc=s_gp)
        B["ident"].w = t
        b.op("gpsimd", lambda e: e.memset(ones_cb.ap(), 1.0), inc=s_gp)
        b.op("gpsimd", lambda e: e.memset(ones_rb.ap(), 1.0), inc=s_gp)
        t3 = b.op("gpsimd", lambda e: e.memset(ones_rf.ap(), 1.0), inc=s_gp)
        B["ones"].w = t3


        d_zf = b.sem("d_zf")
        for ap_d, ap_s in [(convb_d, convb_s), (lng_d, lng_s), (lnb_d, lnb_s),
                           (fcb_d, fcb_s), (cprojb_d, cprojb_s),
                           (b1t_d.rearrange("p e m -> p (e m)"), b1t_s),
                           (wg_d.rearrange("p d e -> p (d e)"), wg_s),
                           (bg_d, bg_s)]:
            dma("sync", ap_s.ap(), ap_d, [], d_sm)
        B["biases"].w = (d_sm, d_sm.n)

        for kc in range(DC):
            dma("gpsimd", win_s.ap()[:, kc * 2 * I:(kc + 1) * 2 * I],
                winT_d[kc * P:(kc + 1) * P, :], [], d_wA)
        for g in range(NG):
            src = convw_d[g].rearrange("k (icc p) o -> p (k icc) o", p=P)
            dst = convw_s.ap()[:, g * 8 * 256:(g + 1) * 8 * 256].rearrange(
                "p (ki o) -> p ki o", o=256)
            dma("gpsimd", dst, src, [], d_wA)
        n_ws_s1 = d_wA.n
        for q in range(4):
            src = fcwT_d[q * 4 * P:(q + 1) * 4 * P, :].rearrange(
                "(kc p) d -> p kc d", p=P)
            dma("gpsimd", fcw_s.ap()[:, q * 4 * D:(q + 1) * 4 * D].rearrange(
                "p (kc d) -> p kc d", d=D), src, [], d_wB)
        n_ws_fc = d_wB.n

        if 1 in stages:
            tz = b.op("gpsimd", lambda e: e.memset(xh_s.ap(), 0.0), inc=s_gp)
            tkh = dma("sync", xh_s.ap()[0:3, :], xh_d, [tz], d_xh)
            B["xh"].w = tkh
            for dc in range(DC):
                g = b.op("tensor", lambda e, dc=dc: e.transpose(
                    pb[6].ap()[:, 0:P], xh_s.ap()[:, dc * P:(dc + 1) * P],
                    ident.ap()),
                    waits=B["xh"].rd() + B["ident"].rd() + PB[6].war(),
                    inc=s_pe)
                PB[6].w = g
                B["xh"].r.append(g)
                ev = b.op("vector", lambda e, dc=dc: e.tensor_copy(
                    xhT_s.ap()[:, dc * 3:(dc + 1) * 3], pb[6].ap()[:, 0:3]),
                    waits=[g] + B["xhT"].war(), inc=s_vec)
                PB[6].r.append(ev)
                B["xhT"].w = ev
            for ch in range(IC):
                wts = B["xhT"].rd() + PB[7].war() + [(d_wA, n_ws_s1)]
                gtok = None
                for kc in range(DC):
                    gtok = b.op("tensor", lambda e, ch=ch, kc=kc: e.matmul(
                        pb[7].ap()[:, 0:3],
                        win_s.ap()[:, kc * 2 * I + ch * P: kc * 2 * I + (ch + 1) * P],
                        xhT_s.ap()[:, kc * 3:(kc + 1) * 3],
                        start=(kc == 0), stop=(kc == DC - 1)),
                        waits=(wts if kc == 0 else []),
                        inc=(s_pe if kc == DC - 1 else None))
                PB[7].w = gtok
                B["xhT"].r.append(gtok)
                B["win"].r.append(gtok)
                ev = b.op("scalar", lambda e, ch=ch: e.copy(
                    hbuf_s.ap()[:, ch * (T1 + 3): ch * (T1 + 3) + 3],
                    pb[7].ap()[:, 0:3]),
                    waits=[gtok] + B["hbuf"].war(), inc=s_act)
                PB[7].r.append(ev)
                B["hbuf"].w = ev

        st1_end = []
        if 1 in stages:
            xtoks = {}

            def emit_xload(ti):
                par = ti % 2
                xbuf = B[f"x{par}"]
                src = x_d[ti * T1:(ti + 1) * T1, :].rearrange(
                    "(tc p) d -> p tc d", p=P)
                tk = dma("sync", x_s[par].ap().rearrange("p (tc d) -> p tc d", d=D),
                         src, xbuf.war(), d_x[par])
                xbuf.wrote(tk)
                xtoks[ti] = tk

            emit_xload(0)
            for ti in range(NT1):
                par = ti % 2
                xs = x_s[par]
                xbuf = B[f"x{par}"]
                xtk = xtoks[ti]
                if ti > 0:
                    hv = b.op("vector", lambda e: e.tensor_copy(
                        hbuf_s.ap().rearrange("p (c j) -> p c j", j=T1 + 3)[:, :, 0:3],
                        hbuf_s.ap().rearrange("p (c j) -> p c j", j=T1 + 3)[:, :, T1:T1 + 3]),
                        waits=B["hbuf"].war(), inc=s_vec)
                    B["hbuf"].wrote(hv)
                for tc in range(2):
                    for dc in range(DC):
                        bank = 4 + ((tc * DC + dc) % 2)
                        g = b.op("tensor", lambda e, tc=tc, dc=dc, bank=bank, xs=xs: e.transpose(
                            pb[bank].ap()[:, 0:P],
                            xs.ap()[:, tc * D + dc * P: tc * D + (dc + 1) * P],
                            ident.ap()),
                            waits=[xtk] + B["ident"].rd() + PB[bank].war() +
                                  (B["xT"].war() if (tc == 0 and dc == 0) else []),
                            inc=s_pe)
                        PB[bank].w = g
                        xbuf.r.append(g)
                        if tc == 0 and dc == 0:
                            B["xT"].r = []
                        ev = b.op("vector", lambda e, tc=tc, dc=dc, bank=bank: e.tensor_copy(
                            xT_s.ap()[:, dc * T1 + tc * P: dc * T1 + (tc + 1) * P],
                            pb[bank].ap()[:, 0:P]),
                            waits=[g], inc=s_vec)
                        PB[bank].r.append(ev)
                        B["xT"].w = ev
                if ti + 1 < NT1:
                    emit_xload(ti + 1)
                for oc in range(2 * IC):
                    bank = oc % 2
                    w0 = B["xT"].rd() + PB[bank].war() + [(d_wA, n_ws_s1)]
                    g = None
                    for kc in range(DC):
                        g = b.op("tensor", lambda e, oc=oc, kc=kc, bank=bank: e.matmul(
                            pb[bank].ap()[:, 0:T1],
                            win_s.ap()[:, kc * 2 * I + oc * P: kc * 2 * I + (oc + 1) * P],
                            xT_s.ap()[:, kc * T1:(kc + 1) * T1],
                            start=(kc == 0), stop=(kc == DC - 1)),
                            waits=(w0 if kc == 0 else []),
                            inc=(s_pe if kc == DC - 1 else None))
                    PB[bank].w = g
                    B["xT"].r.append(g)
                    B["win"].r.append(g)
                    if oc < IC:
                        ev = b.op("scalar", lambda e, oc=oc, bank=bank: e.copy(
                            hbuf_s.ap()[:, oc * (T1 + 3) + 3: oc * (T1 + 3) + 3 + T1],
                            pb[bank].ap()[:, 0:T1]),
                            waits=[g] + B["hbuf"].rd(), inc=s_act)
                        B["hbuf"].r.append(ev)
                    else:
                        ev = b.op("scalar", lambda e, oc=oc, bank=bank: e.activation(
                            gate_s.ap()[:, (oc - IC) * T1:(oc - IC + 1) * T1],
                            pb[bank].ap()[:, 0:T1], AF.Silu),
                            waits=[g] + (B["gate"].war() if oc == IC else []),
                            inc=s_act)
                        if oc == IC:
                            B["gate"].r = []
                        B["gate"].w = ev
                    PB[bank].r.append(ev)
                hev_last = (s_act, s_act.n)
                for gi in range(NG):
                    for occ in range(2):
                        ch = gi * 2 + occ
                        bank = 2 + (ch % 2)
                        w0 = [hev_last] + PB[bank].war() + B["hbuf"].rd()
                        g = None
                        first = True
                        for k in range(KCV):
                            for icc in range(2):
                                last = (k == KCV - 1 and icc == 1)
                                g = b.op("tensor",
                                         lambda e, gi=gi, occ=occ, k=k, icc=icc, bank=bank, first=first: e.matmul(
                                             pb[bank].ap()[:, 0:T1],
                                             convw_s.ap()[:, ((gi * 4 + k) * 2 + icc) * 256 + occ * P:
                                                          ((gi * 4 + k) * 2 + icc) * 256 + (occ + 1) * P],
                                             hbuf_s.ap()[:, (gi * 2 + icc) * (T1 + 3) + k:
                                                         (gi * 2 + icc) * (T1 + 3) + k + T1],
                                             start=first, stop=(k == KCV - 1 and icc == 1)),
                                         waits=(w0 if first else []),
                                         inc=(s_pe if last else None))
                                first = False
                        PB[bank].w = g
                        B["hbuf"].r.append(g)
                        B["convw"].r.append(g)
                        ev = b.op("vector", lambda e, ch=ch, bank=bank: e.scalar_tensor_tensor(
                            u1_s.ap()[:, ch * T1:(ch + 1) * T1],
                            pb[bank].ap()[:, 0:T1],
                            convb_s.ap()[:, ch:ch + 1],
                            gate_s.ap()[:, ch * T1:(ch + 1) * T1],
                            ALU.add, ALU.mult),
                            waits=[g, B["gate"].w, B["biases"].w] +
                                  (B["u1"].war() if ch == 0 else []),
                            inc=s_vec)
                        PB[bank].r.append(ev)
                        B["gate"].r.append(ev)
                        if ch == 0:
                            B["u1"].r = []
                        B["u1"].w = ev
                dstu = u_sp.rearrange("(c p) t -> p c t", p=P)[:, :, ti * T1:(ti + 1) * T1]
                tko = dma("sync", dstu,
                          u1_s.ap().rearrange("p (c t) -> p c t", t=T1),
                          [B["u1"].w], d_uo)
                B["u1"].r.append(tko)
                SP["u"].w = tko
            st1_end = [(s_pe, s_pe.n), (s_act, s_act.n), (s_vec, s_vec.n),
                       (d_uo, d_uo.n)]
            for n in ["fcw", "cprojw", "outw", "u2", "ub", "t1", "hn", "mid",
                      "sqh2", "x2", "s", "x1f2", "x1cs"]:
                B[n].r = list(st1_end)

        if 2 in stages:
            for q in range(2):
                src = cprojwT_d[q * 4 * P:(q + 1) * 4 * P, :].rearrange(
                    "(kc p) d -> p kc d", p=P)
                dma("gpsimd", cprojw_s.ap()[:, q * 4 * I:(q + 1) * 4 * I].rearrange(
                    "p (kc d) -> p kc d", d=I), src, st1_end, d_wC)
            for q in range(4):
                src = outwT_d[q * 4 * P:(q + 1) * 4 * P, :].rearrange(
                    "(kc p) d -> p kc d", p=P)
                dma("gpsimd", outw_s.ap()[:, q * 4 * D:(q + 1) * 4 * D].rearrange(
                    "p (kc d) -> p kc d", d=D), src, st1_end, d_wD)
        n_ws_cp = d_wC.n
        n_ws_all = d_wD.n
        if 3 in stages:
            src = w1_d[0].rearrange("(kc p) m -> p kc m", p=P)[:, :, 0:MMLP // 4]
            dma("gpsimd", w1_s.ap().rearrange(
                "p (q kc m) -> p q kc m", q=4, kc=DC)[:, 0],
                src, st1_end, d_w1q[0])

        st2_end = []
        if 2 in stages:
            utoks = {}
            x2toks = {}

            def emit_uload(ti):
                srcu = u_sp.rearrange("(c p) t -> p c t", p=P)[:, :, ti * T1:(ti + 1) * T1]
                tk = dma("sync", u2_s.ap().rearrange("p (c t) -> p c t", t=T1),
                         srcu, B["u2"].war() + SP["u"].rd(), d_ui)
                B["u2"].wrote(tk)
                utoks[ti] = tk

            def emit_x2load(ti):
                xsrc = x_d[ti * T1:(ti + 1) * T1, :].rearrange("(tc p) d -> p tc d", p=P)
                tk = dma("sync", x2_s.ap().rearrange("p (tc d) -> p tc d", d=D),
                         xsrc, B["x2"].war(), d_x2)
                B["x2"].wrote(tk)
                x2toks[ti] = tk

            emit_uload(0)
            emit_x2load(0)
            s_par = [s_s, x1f2_s]
            B_s = [B["s"], B["x1f2"]]

            def make_tail(ti, x1tok):
                par = ti % 2
                sp = s_par[par]
                sb = B_s[par]

                def tail():
                    dstf = x1f_sp[ti * T1:(ti + 1) * T1, :].rearrange(
                        "(tc p) d -> p tc d", p=P)
                    tkfo = dma("sync", dstf,
                               sp.ap().rearrange("p (tc d) -> p tc d", d=D),
                               [x1tok], d_x1fo)
                    sb.r.append(tkfo)
                    SP["x1f"].w = tkfo
                    if 3 in stages:
                        for rep in range(NE * CAP // t_core):
                            dma("scalar",
                                gathx_sp[rep * t_core + ti * T1:
                                         rep * t_core + (ti + 1) * T1, :],
                                x1f_sp[ti * T1:(ti + 1) * T1, :],
                                [tkfo], d_zf)
                    lg_banks = [2, 3]
                    for dc in range(DC):
                        evs = []
                        for tc in range(2):
                            bank = (dc * 2 + tc) % 2
                            g = b.op("tensor", lambda e, tc=tc, dc=dc, bank=bank, sp=sp: e.transpose(
                                pb[bank].ap()[:, 0:P],
                                sp.ap()[:, tc * D + dc * P: tc * D + (dc + 1) * P],
                                ident.ap()),
                                waits=[x1tok] + PB[bank].war(), inc=s_pe)
                            PB[bank].w = g
                            sb.r.append(g)
                            ev = b.op("vector", lambda e, tc=tc, bank=bank: e.tensor_copy(
                                x1cs_s.ap()[:, tc * P:(tc + 1) * P],
                                pb[bank].ap()[:, 0:P]),
                                waits=[g] + B["x1cs"].war(), inc=s_vec)
                            B["x1cs"].wrote(ev)
                            PB[bank].r.append(ev)
                            evs.append(ev)
                        for tc in range(2):
                            bank = lg_banks[tc]
                            g = b.op("tensor", lambda e, tc=tc, dc=dc, bank=bank: e.matmul(
                                pb[bank].ap()[:, 256:256 + NE],
                                x1cs_s.ap()[:, tc * P:(tc + 1) * P],
                                wg_s.ap()[:, dc * NE:(dc + 1) * NE],
                                start=(dc == 0), stop=False),
                                waits=[evs[tc], B["biases"].w] +
                                      (PB[bank].war() if dc == 0 else []),
                                inc=s_pe)
                            PB[bank].w = g
                            B["x1cs"].r.append(g)
                    for tc in range(2):
                        bank = lg_banks[tc]
                        g = b.op("tensor", lambda e, bank=bank: e.matmul(
                            pb[bank].ap()[:, 256:256 + NE], ones_rf.ap(), bg_s.ap(),
                            start=False, stop=True),
                            waits=[B["ones"].w], inc=s_pe)
                        PB[bank].w = g
                    for tc in range(2):
                        bank = lg_banks[tc]
                        lg = PB[bank].w
                        t_mx = b.op("vector", lambda e, bank=bank: e.reduce_max(
                            sm_s.ap()[:, 16:17], pb[bank].ap()[:, 256:256 + NE],
                            axis=AX.X), waits=[lg] + B["sm"].war(), inc=s_vec)
                        B["sm"].r = []
                        t_sh = b.op("vector", lambda e, bank=bank: e.tensor_scalar(
                            sm_s.ap()[:, 0:NE], pb[bank].ap()[:, 256:256 + NE],
                            sm_s.ap()[:, 16:17], None, ALU.subtract),
                            waits=[t_mx], inc=s_vec)
                        PB[bank].r += [t_mx, t_sh]
                        t_ex = b.op("scalar", lambda e: e.activation(
                            sm_s.ap()[:, 8:8 + NE], sm_s.ap()[:, 0:NE], AF.Exp),
                            waits=[t_sh], inc=s_act)
                        t_sm = b.op("vector", lambda e: e.reduce_sum(
                            sm_s.ap()[:, 17:18], sm_s.ap()[:, 8:8 + NE], axis=AX.X),
                            waits=[t_ex], inc=s_vec)
                        t_rp = b.op("vector", lambda e: e.reciprocal(
                            sm_s.ap()[:, 18:19], sm_s.ap()[:, 17:18]),
                            waits=[t_sm], inc=s_vec)
                        t_im = b.op("vector", lambda e: e.tensor_scalar(
                            sm_s.ap()[:, 0:NE], sm_s.ap()[:, 0:NE], 0.0, None,
                            ALU.is_equal), waits=[t_sh, t_ex, t_rp], inc=s_vec)
                        t_mk = b.op("vector", lambda e: e.tensor_scalar(
                            sm_s.ap()[:, 0:NE], sm_s.ap()[:, 0:NE],
                            sm_s.ap()[:, 18:19], None, ALU.mult),
                            waits=[t_im], inc=s_vec)
                        c = 2 * ti + tc
                        t_wr = b.op("vector", lambda e, c=c: e.tensor_copy(
                            mask_s.ap().rearrange("p (e c) -> p e c", c=NCH)[:, :, c:c + 1],
                            sm_s.ap()[:, 0:NE].rearrange("p (e o) -> p e o", o=1)),
                            waits=[t_mk] + B["mask"].rd(), inc=s_vec)
                        B["sm"].w = t_wr
                        B["mask"].w = t_wr
                return tail

            pend = None
            for ti in range(NT1):
                par = ti % 2
                sp = s_par[par]
                sb = B_s[par]
                utk = utoks[ti]
                xtk = x2toks[ti]
                ubt = b.op("scalar", lambda e: e.copy(ub_s.ap(), u2_s.ap()),
                           waits=[utk] + B["ub"].war(), inc=s_act)
                B["ub"].wrote(ubt)
                B["u2"].r.append(ubt)
                sqt = b.op("vector", lambda e: e.tensor_tensor(
                    sq2_s.ap(), u2_s.ap(), u2_s.ap(), ALU.mult),
                    waits=[utk] + B["sqh2"].war(), inc=s_vec)
                B["sqh2"].wrote(sqt)
                B["u2"].r.append(sqt)
                g1 = None
                for ch in range(IC):
                    g1 = b.op("tensor", lambda e, ch=ch: e.matmul(
                        pb[4].ap()[0:1, 0:T1], ones_cb.ap(),
                        ub_s.ap()[:, ch * T1:(ch + 1) * T1],
                        start=(ch == 0), stop=(ch == IC - 1)),
                        waits=([ubt, B["ones"].w] + PB[4].war()) if ch == 0 else [],
                        inc=(s_pe if ch == IC - 1 else None))
                B["ub"].r.append(g1)
                g2 = None
                for ch in range(IC):
                    g2 = b.op("tensor", lambda e, ch=ch: e.matmul(
                        pb[4].ap()[0:1, T1:2 * T1], ones_cb.ap(),
                        sq2_s.ap()[:, ch * T1:(ch + 1) * T1],
                        start=(ch == 0), stop=(ch == IC - 1)),
                        waits=[sqt] if ch == 0 else [],
                        inc=(s_pe if ch == IC - 1 else None))
                PB[4].w = g2
                B["sqh2"].r.append(g2)
                if pend is not None:
                    pend()
                tmu = b.op("vector", lambda e: e.tensor_scalar(
                    stats_s.ap()[:, 0:T1], pb[4].ap()[0:1, 0:T1],
                    1.0 / I, None, ALU.mult),
                    waits=[g1, g2] + B["stats"].war(), inc=s_vec)
                tms = b.op("vector", lambda e: e.tensor_scalar(
                    stats_s.ap()[:, T1:2 * T1], pb[4].ap()[0:1, T1:2 * T1],
                    1.0 / I, None, ALU.mult), waits=[], inc=s_vec)
                PB[4].r += [tmu, tms]
                tm2 = b.op("vector", lambda e: e.tensor_tensor(
                    stats_s.ap()[:, 2 * T1:3 * T1], stats_s.ap()[:, 0:T1],
                    stats_s.ap()[:, 0:T1], ALU.mult), waits=[tmu], inc=s_vec)
                tvar = b.op("vector", lambda e: e.scalar_tensor_tensor(
                    stats_s.ap()[:, 2 * T1:3 * T1], stats_s.ap()[:, T1:2 * T1],
                    EPS, stats_s.ap()[:, 2 * T1:3 * T1], ALU.add, ALU.subtract),
                    waits=[tms, tm2], inc=s_vec)
                tln = b.op("scalar", lambda e: e.activation(
                    stats_s.ap()[:, 3 * T1:4 * T1], stats_s.ap()[:, 2 * T1:3 * T1],
                    AF.Ln), waits=[tvar], inc=s_act)
                trs = b.op("scalar", lambda e: e.activation(
                    stats_s.ap()[:, 3 * T1:4 * T1], stats_s.ap()[:, 3 * T1:4 * T1],
                    AF.Exp, scale=-0.5), waits=[tln], inc=s_act)
                B["stats"].wrote(trs)
                gb1 = b.op("tensor", lambda e: e.matmul(
                    pb[5].ap()[:, 0:T1], ones_rf.ap(), stats_s.ap()[:, 0:T1],
                    start=True, stop=True),
                    waits=[tmu, trs, B["ones"].w] + PB[5].war(), inc=s_pe)
                gb2 = b.op("tensor", lambda e: e.matmul(
                    pb[5].ap()[:, T1:2 * T1], ones_rf.ap(),
                    stats_s.ap()[:, 3 * T1:4 * T1],
                    start=True, stop=True), waits=[], inc=s_pe)
                PB[5].w = gb2
                B["stats"].r += [gb1, gb2]
                last_th = None
                for ch in range(IC):
                    ta = b.op("vector", lambda e, ch=ch: e.tensor_tensor(
                        t1_s.ap()[:, ch * T1:(ch + 1) * T1],
                        u2_s.ap()[:, ch * T1:(ch + 1) * T1],
                        pb[5].ap()[:, 0:T1], ALU.subtract),
                        waits=[gb1, gb2, utk] + (B["t1"].war() if ch == 0 else []),
                        inc=s_vec)
                    if ch == 0:
                        B["t1"].r = []
                    tb2 = b.op("vector", lambda e, ch=ch: e.tensor_tensor(
                        t1_s.ap()[:, ch * T1:(ch + 1) * T1],
                        t1_s.ap()[:, ch * T1:(ch + 1) * T1],
                        pb[5].ap()[:, T1:2 * T1], ALU.mult),
                        waits=[ta], inc=s_vec)
                    PB[5].r += [ta, tb2]
                    B["t1"].w = tb2
                    B["u2"].r.append(ta)
                    th = b.op("scalar", lambda e, ch=ch: e.activation(
                        hn_s.ap()[:, ch * T1:(ch + 1) * T1],
                        t1_s.ap()[:, ch * T1:(ch + 1) * T1],
                        AF.Identity, bias=lnb_s.ap()[:, ch:ch + 1],
                        scale=lng_s.ap()[:, ch:ch + 1]),
                        waits=[tb2, B["biases"].w] + (B["hn"].war() if ch == 0 else []),
                        inc=s_act)
                    if ch == 0:
                        B["hn"].r = []
                    B["hn"].w = th
                    B["t1"].r.append(th)
                    last_th = th
                hn_all = (s_act, s_act.n)
                for mcc in range(DC):
                    bank = mcc % 2
                    g = None
                    for kc in range(IC):
                        g = b.op("tensor", lambda e, mcc=mcc, kc=kc, bank=bank: e.matmul(
                            pb[bank].ap()[:, 0:T1],
                            fcw_s.ap()[:, kc * D + mcc * P: kc * D + (mcc + 1) * P],
                            hn_s.ap()[:, kc * T1:(kc + 1) * T1],
                            start=(kc == 0), stop=(kc == IC - 1)),
                            waits=([hn_all, (d_wB, n_ws_fc)] + PB[bank].war())
                            if kc == 0 else [],
                            inc=(s_pe if kc == IC - 1 else None))
                    PB[bank].w = g
                    B["hn"].r.append(g)
                    B["fcw"].r.append(g)
                    ev = b.op("scalar", lambda e, mcc=mcc, bank=bank: e.activation(
                        mid_s.ap()[:, mcc * T1:(mcc + 1) * T1],
                        pb[bank].ap()[:, 0:T1], AF.Silu,
                        bias=fcb_s.ap()[:, mcc:mcc + 1]),
                        waits=[g, B["biases"].w] + (B["mid"].war() if mcc == 0 else []),
                        inc=s_act)
                    PB[bank].r.append(ev)
                    if mcc == 0:
                        B["mid"].r = []
                    B["mid"].w = ev
                mid_all = (s_act, s_act.n)
                for ch in range(IC):
                    bank = 2 + ch % 2
                    g = None
                    for kc in range(DC):
                        g = b.op("tensor", lambda e, ch=ch, kc=kc, bank=bank: e.matmul(
                            pb[bank].ap()[:, 0:T1],
                            cprojw_s.ap()[:, kc * I + ch * P: kc * I + (ch + 1) * P],
                            mid_s.ap()[:, kc * T1:(kc + 1) * T1],
                            start=(kc == 0), stop=(kc == DC - 1)),
                            waits=([mid_all, (d_wC, n_ws_cp)] + PB[bank].war())
                            if kc == 0 else [],
                            inc=(s_pe if kc == DC - 1 else None))
                    PB[bank].w = g
                    B["mid"].r.append(g)
                    B["cprojw"].r.append(g)
                    ev = b.op("vector", lambda e, ch=ch, bank=bank: e.scalar_tensor_tensor(
                        h2_s.ap()[:, ch * T1:(ch + 1) * T1],
                        pb[bank].ap()[:, 0:T1],
                        cprojb_s.ap()[:, ch:ch + 1],
                        u2_s.ap()[:, ch * T1:(ch + 1) * T1],
                        ALU.add, ALU.add),
                        waits=[g, B["biases"].w, utk] +
                              (B["sqh2"].war() if ch == 0 else []),
                        inc=s_vec)
                    PB[bank].r.append(ev)
                    B["u2"].r.append(ev)
                    if ch == 0:
                        B["sqh2"].r = []
                    B["sqh2"].w = ev
                h2_all = (s_vec, s_vec.n)
                if ti + 1 < NT1:
                    emit_uload(ti + 1)
                for tc in range(2):
                    for dh in range(2):
                        bank = 4 + tc * 2 + dh
                        g = None
                        for kc in range(IC):
                            g = b.op("tensor", lambda e, tc=tc, dh=dh, kc=kc, bank=bank: e.matmul(
                                pb[bank].ap()[:, 0:512],
                                h2_s.ap()[:, kc * T1 + tc * P: kc * T1 + (tc + 1) * P],
                                outw_s.ap()[:, kc * D + dh * 512: kc * D + (dh + 1) * 512],
                                start=(kc == 0), stop=(kc == IC - 1)),
                                waits=([h2_all, (d_wD, n_ws_all)] + PB[bank].war())
                                if kc == 0 else [],
                                inc=(s_pe if kc == IC - 1 else None))
                        PB[bank].w = g
                        B["sqh2"].r.append(g)
                        B["outw"].r.append(g)
                for tc in range(2):
                    stoks = []
                    for dh in range(2):
                        bank = 4 + tc * 2 + dh
                        tv = b.op("vector", lambda e, tc=tc, dh=dh, bank=bank, sp=sp: e.tensor_tensor(
                            sp.ap()[:, tc * D + dh * 512: tc * D + (dh + 1) * 512],
                            x2_s.ap()[:, tc * D + dh * 512: tc * D + (dh + 1) * 512],
                            pb[bank].ap()[:, 0:512], ALU.add),
                            waits=[PB[bank].w, xtk] +
                                  (sb.war() if (tc == 0 and dh == 0) else []),
                            inc=s_vec)
                        PB[bank].r.append(tv)
                        B["x2"].r.append(tv)
                        if tc == 0 and dh == 0:
                            sb.r = []
                        sb.w = tv
                        stoks.append(tv)
                    c0 = 8 * tc
                    tsq = b.op("vector", lambda e, tc=tc, sp=sp: e.tensor_tensor(
                        sqt2_s.ap(), sp.ap()[:, tc * D:(tc + 1) * D],
                        sp.ap()[:, tc * D:(tc + 1) * D], ALU.mult),
                        waits=stoks + B["t1"].war(), inc=s_vec)
                    B["t1"].wrote(tsq)
                    tr1 = b.op("vector", lambda e, tc=tc, c0=c0, sp=sp: e.reduce_sum(
                        lnsm_s.ap()[:, c0:c0 + 1], sp.ap()[:, tc * D:(tc + 1) * D],
                        axis=AX.X), waits=stoks + B["lnsm"].war(), inc=s_vec)
                    if True:
                        B["lnsm"].r = []
                    tr2 = b.op("vector", lambda e, c0=c0: e.reduce_sum(
                        lnsm_s.ap()[:, c0 + 1:c0 + 2], sqt2_s.ap(),
                        axis=AX.X), waits=[tsq], inc=s_vec)
                    B["t1"].r.append(tr2)
                    tmu2 = b.op("vector", lambda e, c0=c0: e.tensor_scalar(
                        lnsm_s.ap()[:, c0 + 2:c0 + 3], lnsm_s.ap()[:, c0:c0 + 1],
                        1.0 / D, None, ALU.mult), waits=[tr1], inc=s_vec)
                    tms2 = b.op("vector", lambda e, c0=c0: e.tensor_scalar(
                        lnsm_s.ap()[:, c0 + 3:c0 + 4], lnsm_s.ap()[:, c0 + 1:c0 + 2],
                        1.0 / D, None, ALU.mult), waits=[tr2], inc=s_vec)
                    tm22 = b.op("vector", lambda e, c0=c0: e.tensor_tensor(
                        lnsm_s.ap()[:, c0 + 4:c0 + 5], lnsm_s.ap()[:, c0 + 2:c0 + 3],
                        lnsm_s.ap()[:, c0 + 2:c0 + 3], ALU.mult),
                        waits=[tmu2], inc=s_vec)
                    tvr2 = b.op("vector", lambda e, c0=c0: e.scalar_tensor_tensor(
                        lnsm_s.ap()[:, c0 + 5:c0 + 6], lnsm_s.ap()[:, c0 + 3:c0 + 4],
                        EPS, lnsm_s.ap()[:, c0 + 4:c0 + 5], ALU.add, ALU.subtract),
                        waits=[tms2, tm22], inc=s_vec)
                    tl2 = b.op("scalar", lambda e, c0=c0: e.activation(
                        lnsm_s.ap()[:, c0 + 6:c0 + 7], lnsm_s.ap()[:, c0 + 5:c0 + 6],
                        AF.Ln), waits=[tvr2], inc=s_act)
                    tr2b = b.op("scalar", lambda e, c0=c0: e.activation(
                        lnsm_s.ap()[:, c0 + 7:c0 + 8], lnsm_s.ap()[:, c0 + 6:c0 + 7],
                        AF.Exp, scale=-0.5), waits=[tl2], inc=s_act)
                    B["lnsm"].w = tr2b
                    tx1 = b.op("vector", lambda e, tc=tc, c0=c0, sp=sp: e.tensor_scalar(
                        sp.ap()[:, tc * D:(tc + 1) * D],
                        sp.ap()[:, tc * D:(tc + 1) * D],
                        lnsm_s.ap()[:, c0 + 2:c0 + 3],
                        lnsm_s.ap()[:, c0 + 7:c0 + 8],
                        ALU.subtract, ALU.mult),
                        waits=[tr2b], inc=s_vec)
                    B["lnsm"].r.append(tx1)
                    sb.w = tx1
                x1tok = sb.w
                if ti + 1 < NT1:
                    emit_x2load(ti + 1)
                pend = make_tail(ti, x1tok)
            pend()
            st2_end = [(s_pe, s_pe.n), (s_act, s_act.n), (s_vec, s_vec.n),
                       (d_x1fo, d_x1fo.n)]
            for n in ["w1", "w2", "h3", "gxf", "gxT", "ysb_0", "ysb_1",
                      "xrow_0", "xrow_1", "sq3"]:
                B[n].r = list(st2_end)

        if 3 in stages:
            TPE = CAP // T3
            NT3S = NE * TPE
            MQ = MMLP // 4

            rt = []

            def vop(make, waits, eng="vector", inc=None):
                tok = b.op(eng, make, waits=waits, inc=inc or s_vec)
                rt.append(tok)
                return tok

            mtok = B["mask"].w
            st25 = [(s_pe, s_pe.n), (s_act, s_act.n), (s_vec, s_vec.n)]
            t_z = vop(lambda e: e.memset(zeros_s.ap(), 0.0), st25)
            B["zeros"].w = t_z
            t_oh = vop(lambda e: e.tensor_scalar(
                oh_s.ap(), mask_s.ap(), 0.0, None, ALU.is_gt), [mtok] + st25)
            t_p = vop(lambda e: e.tensor_reduce(
                p_tip.ap(),
                mask_s.ap().rearrange("p (e c) -> p c e", e=NE),
                axis=AX.X, op=ALU.add), [mtok])
            B["route"].w = t_p
            scans = []
            for e_i in range(NE):
                tk = vop(lambda e, e_i=e_i: e.tensor_tensor_scan(
                    scn_s.ap()[:, e_i * NCH:(e_i + 1) * NCH],
                    oh_s.ap()[:, e_i * NCH:(e_i + 1) * NCH],
                    zeros_s.ap()[:, 0:NCH], 0.0, ALU.add, ALU.add),
                    [t_oh, B["zeros"].w])
                scans.append(tk)
            t_rows = vop(lambda e: e.tensor_copy(
                rows_s.ap(),
                scn_s.ap().rearrange("p (e c) -> p e c", e=NE)[:, :, NCH - 1:NCH]
                .rearrange("p e o -> p (e o)")), [scans[-1]])
            t_sx = vop(lambda e: e.tensor_tensor(
                scn_s.ap(), scn_s.ap(), oh_s.ap(), ALU.subtract), [t_rows])
            g1 = b.op("tensor", lambda e: e.transpose(
                pb[0].ap()[0:NE, 0:P], rows_s.ap(), ident.ap()),
                waits=[t_rows, B["ident"].w] + PB[0].war(), inc=s_pe)
            PB[0].w = g1
            t_rT = vop(lambda e: e.tensor_copy(rT_s.ap(), pb[0].ap()[0:NE, 0:P]),
                       [g1])
            PB[0].r.append(t_rT)
            t_ri = vop(lambda e: e.tensor_tensor_scan(
                rTi_s.ap(), rT_s.ap(), zeros_s.ap()[0:NE, 0:P],
                0.0, ALU.add, ALU.add), [t_rT])
            t_rx = vop(lambda e: e.tensor_tensor(
                rTi_s.ap(), rTi_s.ap(), rT_s.ap(), ALU.subtract), [t_ri])
            g2 = b.op("tensor", lambda e: e.transpose(
                pb[0].ap()[:, 0:NE], rTi_s.ap(), ident.ap()[0:NE, 0:NE]),
                waits=[t_rx] + PB[0].war(), inc=s_pe)
            PB[0].w = g2
            t_po = vop(lambda e: e.tensor_copy(poff_s.ap(), pb[0].ap()[:, 0:NE]),
                       [g2])
            PB[0].r.append(t_po)
            t_prev = t_po
            for e_i in range(NE):
                t_a = vop(lambda e, e_i=e_i: e.tensor_scalar(
                    scn_s.ap()[:, e_i * NCH:(e_i + 1) * NCH],
                    scn_s.ap()[:, e_i * NCH:(e_i + 1) * NCH],
                    poff_s.ap()[:, e_i:e_i + 1], float(e_i * CAP),
                    ALU.add, ALU.add), [t_po, t_sx, t_prev])
                t_b = vop(lambda e, e_i=e_i: e.tensor_scalar(
                    tmpc_s.ap(), scn_s.ap()[:, e_i * NCH:(e_i + 1) * NCH],
                    float((e_i + 1) * CAP), None, ALU.is_ge), [t_a])
                t_c = vop(lambda e, e_i=e_i: e.scalar_tensor_tensor(
                    scn_s.ap()[:, e_i * NCH:(e_i + 1) * NCH],
                    tmpc_s.ap(), 1.0e9,
                    scn_s.ap()[:, e_i * NCH:(e_i + 1) * NCH],
                    ALU.mult, ALU.add), [t_b])
                t_prev = t_c
            t_d0 = vop(lambda e: e.memset(dest_s.ap(), 0.0), [t_prev])
            t_acc = t_d0
            for e_i in range(NE):
                t_m = vop(lambda e, e_i=e_i: e.tensor_tensor(
                    tmpc_s.ap(), oh_s.ap()[:, e_i * NCH:(e_i + 1) * NCH],
                    scn_s.ap()[:, e_i * NCH:(e_i + 1) * NCH], ALU.mult),
                    [t_acc])
                t_acc = vop(lambda e: e.tensor_tensor(
                    dest_s.ap(), dest_s.ap(), tmpc_s.ap(), ALU.add), [t_m])
            t_di = vop(lambda e: e.tensor_copy(dest_i.ap(), dest_s.ap()), [t_acc])

            bnd_reg = nc.gpsimd.alloc_register("bnd")
            b.op("gpsimd", lambda e: e.reg_mov(bnd_reg, NSLOT - 1), waits=[])
            for c in range(NCH):
                par = c % 2
                xr = xrow_s[par]
                xrb = B[f"xrow_{par}"]
                tk = dma("sync", xr.ap(),
                         x1f_sp[c * P:(c + 1) * P, :],
                         xrb.war() + SP["x1f"].rd(), d_xr[par])
                xrb.wrote(tk)
                sc = b.op("gpsimd", lambda e, c=c, xr=xr: e.indirect_dma_start(
                    gathx_sp, IndirectOffsetOnAxis(dest_i.ap()[:, c:c + 1], 0),
                    xr.ap(), None,
                    bounds_check=bnd_reg, oob_is_err=False),
                    waits=[tk, t_di, (d_zf, d_zf.n)], inc=d_sc[par], inck=16)
                xrb.r.append(sc)
            sc_all = [(d_sc[0], d_sc[0].n), (d_sc[1], d_sc[1].n)]

            def w1_src(e_i, q):
                return w1_d[e_i].rearrange("(kc p) m -> p kc m", p=P)[
                    :, :, q * MQ:(q + 1) * MQ]

            def w1_dst(q):
                return w1_s.ap().rearrange("p (q kc m) -> p q kc m", q=4, kc=DC)[:, q]

            def w2_src(e_i, dh, half):
                return w2_d[e_i].rearrange("(mc p) d -> p mc d", p=P)[
                    :, half * 16:(half + 1) * 16, dh * 512:(dh + 1) * 512]

            def w2_dst(dh, half):
                return w2_s.ap()[:, dh * MC * 512 + half * 16 * 512:
                                 dh * MC * 512 + (half + 1) * 16 * 512].rearrange(
                    "p (mc j) -> p mc j", j=512)

            for q in range(1, 4):
                dma("gpsimd", w1_dst(q), w1_src(0, q), st2_end, d_w1q[q])
            for dh in range(2):
                for half in range(2):
                    dma("gpsimd", w2_dst(dh, half), w2_src(0, dh, half),
                        st2_end, d_w2q[dh * 2 + half])
            dma("gpsimd", b2r_s.ap(), b2_d[0:1, :], st2_end + B["b2r"].war(), d_wb)

            gxtoks = {}

            def emit_gxload(k):
                tk = dma("sync", gxf_s.ap().rearrange("p (tc d) -> p tc d", d=D),
                         gathx_sp[k * T3:(k + 1) * T3, :].rearrange(
                             "(tc p) d -> p tc d", p=P),
                         B["gxf"].war() + sc_all, d_gx)
                B["gxf"].wrote(tk)
                gxtoks[k] = tk

            emit_gxload(0)
            ysb_par = 0
            for e_i in range(NE):
                w1_need = {q: (d_w1q[q], 16 * (e_i + 1)) for q in range(4)}
                w2_need = {dh: [(d_w2q[dh * 2], 16 * (e_i + 1)),
                                (d_w2q[dh * 2 + 1], 16 * (e_i + 1))]
                           for dh in range(2)}
                b2_need = [(d_wb, 16 * (e_i + 1))]
                w1_q_last = {}
                w2_dh_last = {}
                b2_readers = []
                b2b = {}
                for dh in range(2):
                    gb = b.op("tensor", lambda e, dh=dh: e.matmul(
                        pb[6].ap()[:, 0:512], ones_rb.ap(),
                        b2r_s.ap()[0:1, dh * 512:(dh + 1) * 512],
                        start=True, stop=True),
                        waits=[B["ones"].w] + b2_need + PB[6].war(),
                        inc=s_pe)
                    PB[6].w = gb
                    b2_readers.append(gb)
                    evb = b.op("vector", lambda e, dh=dh: e.tensor_copy(
                        b2b_s.ap()[:, dh * 512:(dh + 1) * 512],
                        pb[6].ap()[:, 0:512]),
                        waits=[gb] + (B["b2r"].war() if dh == 0 else []),
                        inc=s_vec)
                    PB[6].r.append(evb)
                    b2b[dh] = evb
                for tpe_i in range(TPE):
                    k = e_i * TPE + tpe_i
                    gxtk = gxtoks[k]
                    for tc in range(4):
                        for dc in range(DC):
                            bank = 6 + ((tc * DC + dc) % 2)
                            g = b.op("tensor", lambda e, tc=tc, dc=dc, bank=bank: e.transpose(
                                pb[bank].ap()[:, 0:P],
                                gxf_s.ap()[:, tc * D + dc * P: tc * D + (dc + 1) * P],
                                ident.ap()),
                                waits=[gxtk, B["ident"].w] + PB[bank].war() +
                                      (B["gxT"].war() if (tc == 0 and dc == 0) else []),
                                inc=s_pe)
                            PB[bank].w = g
                            B["gxf"].r.append(g)
                            if tc == 0 and dc == 0:
                                B["gxT"].r = []
                            ev = b.op("vector", lambda e, tc=tc, dc=dc, bank=bank: e.tensor_copy(
                                gxT_s.ap()[:, dc * T3 + tc * P: dc * T3 + (tc + 1) * P],
                                pb[bank].ap()[:, 0:P]),
                                waits=[g], inc=s_vec)
                            PB[bank].r.append(ev)
                            B["gxT"].w = ev
                    gxT_all = (s_vec, s_vec.n)
                    H_BANKS = [0, 1, 6, 7]
                    for mcc in range(MC):
                        q, j = mcc // 8, mcc % 8
                        bank = H_BANKS[mcc % 4]
                        pbw = PB[bank].war()
                        g = None
                        for kc in range(DC):
                            g = b.op("tensor", lambda e, q=q, j=j, kc=kc, bank=bank: e.matmul(
                                pb[bank].ap()[:, 0:T3],
                                w1_s.ap()[:, (q * DC + kc) * MQ + j * P:
                                          (q * DC + kc) * MQ + (j + 1) * P],
                                gxT_s.ap()[:, kc * T3:(kc + 1) * T3],
                                start=(kc == 0), stop=(kc == DC - 1)),
                                waits=([gxT_all, w1_need[q]] + pbw +
                                       (B["h3"].war() if mcc == 0 else []))
                                if kc == 0 else [],
                                inc=(s_pe if kc == DC - 1 else None))
                        PB[bank].w = g
                        B["gxT"].r.append(g)
                        w1_q_last[q] = g
                        if mcc == 0:
                            B["h3"].r = []
                        if mcc % 2 == 0:
                            ev = b.op("scalar", lambda e, mcc=mcc, bank=bank, e_i=e_i: e.activation(
                                h3_s.ap()[:, mcc * T3:(mcc + 1) * T3],
                                pb[bank].ap()[:, 0:T3], AF.Relu,
                                bias=b1t_s.ap()[:, e_i * MC + mcc:e_i * MC + mcc + 1]),
                                waits=[g, B["biases"].w], inc=s_act)
                        else:
                            ev = b.op("vector", lambda e, mcc=mcc, bank=bank, e_i=e_i: e.tensor_scalar(
                                h3_s.ap()[:, mcc * T3:(mcc + 1) * T3],
                                pb[bank].ap()[:, 0:T3],
                                b1t_s.ap()[:, e_i * MC + mcc:e_i * MC + mcc + 1],
                                0.0, ALU.add, ALU.max),
                                waits=[g, B["biases"].w], inc=s_vec)
                        PB[bank].r.append(ev)
                        B["h3"].w = ev
                    h3_all_a = (s_act, s_act.n)
                    h3_all_v = (s_vec, s_vec.n)
                    if k + 1 < NT3S:
                        emit_gxload(k + 1)
                    for dh in range(2):
                        for tc in range(4):
                            gidx = dh * 4 + tc
                            bank = 2 + gidx % 4
                            pbw = PB[bank].war()
                            g = None
                            for mcc in range(MC):
                                g = b.op("tensor", lambda e, tc=tc, dh=dh, mcc=mcc, bank=bank: e.matmul(
                                    pb[bank].ap()[:, 0:512],
                                    h3_s.ap()[:, mcc * T3 + tc * P: mcc * T3 + (tc + 1) * P],
                                    w2_s.ap()[:, dh * MC * 512 + mcc * 512:
                                              dh * MC * 512 + (mcc + 1) * 512],
                                    start=(mcc == 0), stop=(mcc == MC - 1)),
                                    waits=([h3_all_a, h3_all_v] + w2_need[dh] + pbw)
                                    if mcc == 0 else [],
                                    inc=(s_pe if mcc == MC - 1 else None))
                            PB[bank].w = g
                            w2_dh_last[dh] = g
                            B["h3"].r.append(g)
                            par = ysb_par
                            ysb_par ^= 1
                            yb = B[f"ysb_{par}"]
                            ev = b.op("vector",
                                      lambda e, bank=bank, dh=dh, par=par:
                                      e.tensor_tensor(
                                          ysb_s[par].ap(), pb[bank].ap()[:, 0:512],
                                          b2b_s.ap()[:, dh * 512:(dh + 1) * 512],
                                          ALU.add),
                                      waits=[g, b2b[dh]] + yb.war(), inc=s_vec)
                            PB[bank].r.append(ev)
                            B["b2r"].r.append(ev)
                            yb.wrote(ev)
                            dsty = ygath_sp[k * T3 + tc * P: k * T3 + (tc + 1) * P,
                                            dh * 512:(dh + 1) * 512]
                            tko = dma("sync", dsty, ysb_s[par].ap(), [ev], d_yo[par])
                            yb.r.append(tko)
                if e_i < NE - 1:
                    for q in range(4):
                        dma("gpsimd", w1_dst(q), w1_src(e_i + 1, q),
                            [w1_q_last[q]], d_w1q[q])
                    for dh in range(2):
                        for half in range(2):
                            dma("gpsimd", w2_dst(dh, half), w2_src(e_i + 1, dh, half),
                                [w2_dh_last[dh]], d_w2q[dh * 2 + half])
                    dma("gpsimd", b2r_s.ap(), b2_d[e_i + 1:e_i + 2, :],
                        [max(b2_readers, key=lambda t: t[1])], d_wb)

            yo_all = [(d_yo[0], d_yo[0].n), (d_yo[1], d_yo[1].n)]
            st3_end = [(s_pe, s_pe.n), (s_act, s_act.n), (s_vec, s_vec.n)]
            for n in ["yt_0", "yt_1", "x1ft_0", "x1ft_1"]:
                B[n].r = list(st3_end) + [max(
                    list(w1_q_last.values()) + list(w2_dh_last.values()),
                    key=lambda t: t[1])]

            NTF = t_core // T3
            for tau in range(NTF):
                par = tau % 2
                yt = yt_s[par]
                ytb = B[f"yt_{par}"]
                xf = x1ft_s[par]
                xfb = B[f"x1ft_{par}"]
                tz = b.op("vector", lambda e, yt=yt: e.memset(yt.ap(), 0.0),
                          waits=ytb.war(), inc=s_vec)
                ytb.wrote(tz)
                gts = []
                for tc in range(4):
                    c = 4 * tau + tc
                    gt = b.op("gpsimd", lambda e, yt=yt, tc=tc, c=c: e.indirect_dma_start(
                        yt.ap()[:, tc * D:(tc + 1) * D], None,
                        ygath_sp, IndirectOffsetOnAxis(dest_i.ap()[:, c:c + 1], 0),
                        bounds_check=bnd_reg, oob_is_err=False),
                        waits=[tz, t_di] + yo_all, inc=d_yg, inck=16)
                    gts.append(gt)
                ytb.wrote((d_yg, d_yg.n))
                xtk = dma("sync", xf.ap().rearrange("p (tc d) -> p tc d", d=D),
                          x1f_sp[tau * T3:(tau + 1) * T3, :].rearrange(
                              "(tc p) d -> p tc d", p=P),
                          xfb.war() + SP["x1f"].rd(), d_xf[par])
                xfb.wrote(xtk)
                fus = []
                for tc in range(4):
                    c = 4 * tau + tc
                    tf0 = b.op("vector", lambda e, xf=xf, yt=yt, tc=tc, c=c:
                               e.scalar_tensor_tensor(
                                   xf.ap()[:, tc * D:(tc + 1) * D],
                                   yt.ap()[:, tc * D:(tc + 1) * D],
                                   p_tip.ap()[:, c:c + 1],
                                   xf.ap()[:, tc * D:(tc + 1) * D],
                                   ALU.mult, ALU.add),
                               waits=[xtk, ytb.w, B["route"].w], inc=s_vec)
                    xfb.wrote(tf0)
                    ytb.r.append(tf0)
                    fus.append(tf0)
                acc_done = fus[-1]
                fin = []
                ln_lane = {}
                for tc in range(4):
                    c0 = 16 + 8 * tc
                    a_sl = xf.ap()[:, tc * D:(tc + 1) * D]
                    tq = b.op("vector", lambda e, a_sl=a_sl: e.tensor_tensor(
                        sq3_s.ap(), a_sl, a_sl, ALU.mult),
                        waits=[acc_done] + B["sq3"].war(), inc=s_vec)
                    t1r = b.op("vector", lambda e, a_sl=a_sl, c0=c0: e.reduce_sum(
                        lnsm_s.ap()[:, c0:c0 + 1], a_sl, axis=AX.X),
                        waits=[acc_done], inc=s_vec)
                    t2r = b.op("vector", lambda e, c0=c0: e.reduce_sum(
                        lnsm_s.ap()[:, c0 + 1:c0 + 2], sq3_s.ap(), axis=AX.X),
                        waits=[tq], inc=s_vec)
                    B["sq3"].wrote(tq)
                    B["sq3"].r.append(t2r)
                    tmu3 = b.op("vector", lambda e, c0=c0: e.tensor_scalar(
                        lnsm_s.ap()[:, c0 + 2:c0 + 3], lnsm_s.ap()[:, c0:c0 + 1],
                        1.0 / D, None, ALU.mult), waits=[t1r], inc=s_vec)
                    tms3 = b.op("vector", lambda e, c0=c0: e.tensor_scalar(
                        lnsm_s.ap()[:, c0 + 3:c0 + 4], lnsm_s.ap()[:, c0 + 1:c0 + 2],
                        1.0 / D, None, ALU.mult), waits=[t2r], inc=s_vec)
                    tm23 = b.op("vector", lambda e, c0=c0: e.tensor_tensor(
                        lnsm_s.ap()[:, c0 + 4:c0 + 5], lnsm_s.ap()[:, c0 + 2:c0 + 3],
                        lnsm_s.ap()[:, c0 + 2:c0 + 3], ALU.mult), waits=[tmu3], inc=s_vec)
                    tvr3 = b.op("vector", lambda e, c0=c0: e.scalar_tensor_tensor(
                        lnsm_s.ap()[:, c0 + 5:c0 + 6], lnsm_s.ap()[:, c0 + 3:c0 + 4],
                        EPS, lnsm_s.ap()[:, c0 + 4:c0 + 5], ALU.add, ALU.subtract),
                        waits=[tms3, tm23], inc=s_vec)
                    ln_lane[tc] = tvr3
                for tc in range(4):
                    c0 = 16 + 8 * tc
                    tl3 = b.op("scalar", lambda e, c0=c0: e.activation(
                        lnsm_s.ap()[:, c0 + 6:c0 + 7], lnsm_s.ap()[:, c0 + 5:c0 + 6],
                        AF.Ln), waits=[ln_lane[tc]], inc=s_act)
                    tr3 = b.op("scalar", lambda e, c0=c0: e.activation(
                        lnsm_s.ap()[:, c0 + 7:c0 + 8], lnsm_s.ap()[:, c0 + 6:c0 + 7],
                        AF.Exp, scale=-0.5), waits=[tl3], inc=s_act)
                    a_sl = xf.ap()[:, tc * D:(tc + 1) * D]
                    tf = b.op("vector", lambda e, a_sl=a_sl, c0=c0: e.tensor_scalar(
                        a_sl, a_sl, lnsm_s.ap()[:, c0 + 2:c0 + 3],
                        lnsm_s.ap()[:, c0 + 7:c0 + 8], ALU.subtract, ALU.mult),
                        waits=[tr3], inc=s_vec)
                    fin.append(tf)
                B["lnsm"].w = fin[-1]
                B["lnsm"].r = []
                xfb.wrote(fin[-1])
                dsto = out_d[tau * T3:(tau + 1) * T3, :].rearrange(
                    "(tc p) d -> p tc d", p=P)
                tko = dma("sync", dsto,
                          xf.ap().rearrange("p (tc d) -> p tc d", d=D),
                          fin, d_out[par])
                xfb.r.append(tko)

            b.op("sync", lambda e: e.nop(),
                 waits=[(d_out[0], d_out[0].n), (d_out[1], d_out[1].n)])

        with nc.Block() as blk:
            @blk.tensor
            def _(e):
                b.replay("tensor", e)

            @blk.vector
            def _(e):
                b.replay("vector", e)

            @blk.scalar
            def _(e):
                b.replay("scalar", e)

            @blk.gpsimd
            def _(e):
                b.replay("gpsimd", e)

            @blk.sync
            def _(e):
                b.replay("sync", e)

    return nc


def _host_layout(inputs, t_core=4096, n_cores=8):
    x = np.ascontiguousarray(inputs["L0_attn_hidden_states"].reshape(-1, D))

    def pcol(v, nch):
        return np.ascontiguousarray(np.asarray(v, np.float32).reshape(nch, P).T)

    shared = dict(
        winT=np.ascontiguousarray(np.asarray(inputs["L0_attn_in_proj_w"], np.float32).T),
        convw=np.ascontiguousarray(
            np.asarray(inputs["L0_attn_conv_w"], np.float32)
            .reshape(NG, 256, 256, KCV).transpose(0, 3, 2, 1)),
        convb=pcol(inputs["L0_attn_conv_b"], IC),
        lng=pcol(inputs["L0_attn_ln_g"], IC),
        lnb=pcol(inputs["L0_attn_ln_b"], IC),
        fcwT=np.ascontiguousarray(np.asarray(inputs["L0_attn_fc_w"], np.float32).T),
        fcb=pcol(inputs["L0_attn_fc_b"], DC),
        cprojwT=np.ascontiguousarray(np.asarray(inputs["L0_attn_cproj_w"], np.float32).T),
        cprojb=pcol(inputs["L0_attn_cproj_b"], IC),
        outwT=np.ascontiguousarray(np.asarray(inputs["L0_attn_out_w"], np.float32).T),
        wg=np.ascontiguousarray(
            np.asarray(inputs["L0_ffn_w_gate"], np.float32)
            .reshape(DC, P, NE).transpose(1, 0, 2)),
        bg=np.ascontiguousarray(np.asarray(inputs["L0_ffn_b_gate"], np.float32).reshape(1, NE)),
        w1=np.ascontiguousarray(np.asarray(inputs["L0_ffn_W1"], np.float32)),
        b1t=np.ascontiguousarray(
            np.asarray(inputs["L0_ffn_b1"], np.float32)
            .reshape(NE, MC, P).transpose(2, 0, 1)),
        w2=np.ascontiguousarray(np.asarray(inputs["L0_ffn_W2"], np.float32)),
        b2=np.ascontiguousarray(np.asarray(inputs["L0_ffn_b2"], np.float32)),
    )
    in_maps = []
    zero_h = np.zeros((3, D), np.float32)
    for c in range(n_cores):
        lo = c * t_core
        m = dict(shared)
        m["x"] = np.ascontiguousarray(x[lo:lo + t_core])
        if lo % 8192 != 0 and lo >= 3:
            m["xh"] = np.ascontiguousarray(x[lo - 3:lo])
        else:
            m["xh"] = zero_h
        in_maps.append(m)
    return in_maps


def kernel(**inputs):
    t_core = 4096
    in_maps = _host_layout(inputs, t_core=t_core, n_cores=8)
    nc = build_program(t_core=t_core, dbg=DBG)
    res = run_bass_kernel_spmd(nc, in_maps, core_ids=list(range(8)))
    out = np.stack([res.results[c]["out"] for c in range(8)], axis=0)
    return out.reshape(4, 8192, D).astype(np.float32)


# version 13
# speedup vs baseline: 1.0894x; 1.0894x over previous
import numpy as np
import concourse.bass as bass
from concourse.bass import IndirectOffsetOnAxis
import concourse.mybir as mybir
from concourse.bass_utils import run_bass_kernel_spmd

F32 = mybir.dt.float32
BF16 = mybir.dt.bfloat16
AF = mybir.ActivationFunctionType
ALU = mybir.AluOpType
AX = mybir.AxisListType

P = 128
D = 1024
I = 2048
NG = 8
KCV = 4
MMLP = 4096
NE = 8
DC = D // P
IC = I // P
MC = MMLP // P
EPS = 1e-5
CAP = 1024

DBG = False


class _Sem:
    __slots__ = ("h", "n")

    def __init__(self, h):
        self.h = h
        self.n = 0


class _Buf:
    __slots__ = ("w", "r")

    def __init__(self):
        self.w = None
        self.r = []

    def rd(self):
        return [self.w] if self.w is not None else []

    def war(self):
        out = list(self.r)
        if self.w is not None:
            out.append(self.w)
        return out

    def wrote(self, tok):
        self.w = tok
        self.r = []


class _B:
    def __init__(self, nc, es):
        self.nc = nc
        self.es = es
        self.items = []
        self.waited = {}

    def sem(self, name):
        return _Sem(self.es.enter_context(self.nc.semaphore(name)))

    def op(self, eng, make, waits=(), inc=None, inck=1):
        best = {}
        for s, v in waits:
            if s is None or v <= 0:
                continue
            assert v <= s.n, f"wait {v} > built count {s.n}"
            k = id(s)
            if k not in best or best[k][1] < v:
                best[k] = (s, v)
        wl = []
        for s, v in best.values():
            key = (eng, id(s))
            if self.waited.get(key, 0) >= v:
                continue
            self.waited[key] = v
            wl.append((s.h, v))
        tok = None
        if inc is not None:
            inc.n += inck
            tok = (inc, inc.n)

        def fn(e, make=make, wl=wl, inc=inc, inck=inck):
            for h, v in wl:
                e.wait_ge(h, v)
            ins = make(e)
            if inc is not None:
                ins.then_inc(inc.h, inck)

        self.items.append((eng, fn))
        return tok

    def replay(self, name, e):
        for eng, fn in self.items:
            if eng == name:
                fn(e)


def build_program(t_core=4096, stages=(1, 2, 3), dbg=False):
    T1 = 256
    T3 = 512
    NT1 = t_core // T1
    NT3 = t_core // T3
    nc = bass.Bass("TRN2", target_bir_lowering=False, debug=False)

    x_d = nc.dram_tensor("x", [t_core, D], F32, kind="ExternalInput").ap()
    xh_d = nc.dram_tensor("xh", [3, D], F32, kind="ExternalInput").ap()
    winT_d = nc.dram_tensor("winT", [D, 2 * I], F32, kind="ExternalInput").ap()
    convw_d = nc.dram_tensor("convw", [NG, KCV, 256, 256], F32, kind="ExternalInput").ap()
    convb_d = nc.dram_tensor("convb", [P, IC], F32, kind="ExternalInput").ap()
    lng_d = nc.dram_tensor("lng", [P, IC], F32, kind="ExternalInput").ap()
    lnb_d = nc.dram_tensor("lnb", [P, IC], F32, kind="ExternalInput").ap()
    fcwT_d = nc.dram_tensor("fcwT", [I, D], F32, kind="ExternalInput").ap()
    fcb_d = nc.dram_tensor("fcb", [P, DC], F32, kind="ExternalInput").ap()
    cprojwT_d = nc.dram_tensor("cprojwT", [D, I], F32, kind="ExternalInput").ap()
    cprojb_d = nc.dram_tensor("cprojb", [P, IC], F32, kind="ExternalInput").ap()
    outwT_d = nc.dram_tensor("outwT", [I, D], F32, kind="ExternalInput").ap()
    wg_d = nc.dram_tensor("wg", [P, DC, NE], F32, kind="ExternalInput").ap()
    bg_d = nc.dram_tensor("bg", [1, NE], F32, kind="ExternalInput").ap()
    w1_d = nc.dram_tensor("w1", [NE, D, MMLP], F32, kind="ExternalInput").ap()
    b1t_d = nc.dram_tensor("b1t", [P, NE, MC], F32, kind="ExternalInput").ap()
    w2_d = nc.dram_tensor("w2", [NE, MMLP, D], F32, kind="ExternalInput").ap()
    b2_d = nc.dram_tensor("b2", [NE, D], F32, kind="ExternalInput").ap()
    out_d = nc.dram_tensor("out", [t_core, D], F32, kind="ExternalOutput").ap()

    NSLOT = NE * CAP
    sk = "ExternalOutput" if dbg else "Internal"
    u_sp = nc.dram_tensor("u_sp", [I, t_core], F32, kind=sk).ap()
    x1f_sp = nc.dram_tensor("x1f_sp", [t_core, D], F32, kind=sk).ap()
    gathx_sp = nc.dram_tensor("gathx_sp", [NSLOT, D], F32, kind="Internal").ap()
    ygath_sp = nc.dram_tensor("ygath_sp", [NSLOT, D], F32, kind=sk).ap()

    import contextlib
    es = contextlib.ExitStack()
    with es:
        b = _B(nc, es)

        def sb(name, shape, dt):
            return es.enter_context(nc.sbuf_tensor(name, shape, dt))

        ident = sb("ident", [P, P], F32)
        ones_cb = sb("ones_cb", [P, 1], BF16)
        ones_rb = sb("ones_rb", [1, P], BF16)
        ones_rf = sb("ones_rf", [1, P], F32)
        convb_s = sb("convb_s", [P, IC], F32)
        lng_s = sb("lng_s", [P, IC], F32)
        lnb_s = sb("lnb_s", [P, IC], F32)
        fcb_s = sb("fcb_s", [P, DC], F32)
        cprojb_s = sb("cprojb_s", [P, IC], F32)
        b1t_s = sb("b1t_s", [P, NE * MC], F32)
        wg_s = sb("wg_s", [P, DC * NE], F32)
        bg_s = sb("bg_s", [1, NE], F32)
        b2r_s = sb("b2r_s", [1, D], BF16)
        NCH = t_core // P
        mask_s = sb("mask_s", [P, NE * NCH], F32)
        lnsm_s = sb("lnsm_s", [P, 48], F32)
        rows_s = sb("rows_s", [P, NE], F32)
        poff_s = sb("poff_s", [P, NE], F32)
        tmpc_s = sb("tmpc_s", [P, NCH], F32)
        dest_s = sb("dest_s", [P, NCH], F32)
        dest_i = sb("dest_i", [P, NCH], mybir.dt.int32)
        p_tip = sb("p_tip", [P, NCH], F32)
        sm_s = sb("sm_s", [P, 32], F32)

        ov0 = (nc.sbuf_base + 31) // 32 * 32
        lim = nc.sbuf_top - ov0
        KB = 1024

        def at(name, shape, dt, off):
            esz = 2 if dt == BF16 else 4
            need = off + shape[1] * esz
            assert need <= lim, f"{name}: {need} > {lim}"
            return nc.alloc_sbuf_tensor_at(name, shape, dt, offset=ov0 + off)

        w1_s = at("w1_s", [P, 4 * DC * (MMLP // 4)], BF16, 0)
        w2_s = at("w2_s", [P, MC * D], BF16, 64 * KB)
        h3_s = at("h3_s", [P, MC * T3], BF16, 128 * KB)
        gxf_s = at("gxf_s", [P, 4 * D], F32, 160 * KB)
        gxT_s = at("gxT_s", [P, DC * T3], BF16, 176 * KB)
        xrow_s = [at(f"xrow_{i}", [P, D], F32, 176 * KB + i * 4 * KB)
                  for i in range(2)]
        ysb_s = [at(f"ysb_{i}", [P, 512], F32, 184 * KB + i * 2 * KB)
                 for i in range(2)]
        sq3_s = at("sq3_s", [P, D], F32, 188 * KB)
        b2b_s = at("b2b_s", [P, D], BF16, 192 * KB)
        oh_s = at("oh_s", [P, NE * NCH], F32, 128 * KB)
        scn_s = at("scn_s", [P, NE * NCH], F32, 129 * KB)
        zeros_s = at("zeros_s", [P, P], F32, 130 * KB)
        rT_s = at("rT_s", [NE, P], F32, 131 * KB)
        rTi_s = at("rTi_s", [NE, P], F32, 131 * KB + 512)
        yt_s = [at(f"yt_{i}", [P, 4 * D], F32, i * 16 * KB) for i in range(2)]
        x1ft_s = [at(f"x1ft_{i}", [P, 4 * D], F32, 32 * KB + i * 16 * KB)
                  for i in range(2)]

        def lay2(pairs, base):
            out, off = {}, base
            for name, nbytes in pairs:
                out[name] = off
                off += (nbytes + 31) // 32 * 32
            assert off <= lim, f"stage2 overflow {off} > {lim}"
            return out

        o2 = lay2([
            ("fcw", IC * D * 2), ("cprojw", DC * I * 2), ("outw", IC * D * 2),
            ("u2", IC * T1 * 4), ("ub", IC * T1 * 2), ("t1", IC * T1 * 4),
            ("hn", IC * T1 * 2), ("mid", DC * T1 * 2), ("sqh2", IC * T1 * 2),
            ("x2", 2 * D * 4), ("s", 2 * D * 4), ("x1f2", 2 * D * 4),
            ("stats", 4 * T1 * 4),
        ], 16 * KB)
        fcw_s = at("fcw_s", [P, IC * D], BF16, o2["fcw"])
        cprojw_s = at("cprojw_s", [P, DC * I], BF16, o2["cprojw"])
        outw_s = at("outw_s", [P, IC * D], BF16, o2["outw"])
        u2_s = at("u2_s", [P, IC * T1], F32, o2["u2"])
        ub_s = at("ub_s", [P, IC * T1], BF16, o2["ub"])
        t1_s = at("t1_s", [P, IC * T1], F32, o2["t1"])
        sqt2_s = at("sqt2_s", [P, D], F32, o2["t1"])
        hn_s = at("hn_s", [P, IC * T1], BF16, o2["hn"])
        mid_s = at("mid_s", [P, DC * T1], BF16, o2["mid"])
        sq2_s = at("sq2_s", [P, IC * T1], BF16, o2["sqh2"])
        h2_s = at("h2_s", [P, IC * T1], BF16, o2["sqh2"])
        x2_s = at("x2_s", [P, 2 * D], F32, o2["x2"])
        s_s = at("s_s", [P, 2 * D], F32, o2["s"])
        x1f2_s = at("x1f2_s", [P, 2 * D], F32, o2["x1f2"])
        x1cs_s = at("x1cs_s", [P, T1], F32, o2["stats"])
        stats_s = at("stats_s", [1, 4 * T1], F32, o2["stats"])
        x1b2_s = at("x1b2_s", [P, DC * T1], BF16, o2["mid"])

        o1 = lay2([
            ("x0", 2 * D * 4), ("x1", 2 * D * 4),
            ("fcw_hole", IC * D * 2),
            ("win", DC * 2 * I * 2), ("convw", 64 * 256 * 2),
            ("hbuf", IC * (T1 + 3) * 2), ("gate", IC * T1 * 4),
            ("u1", IC * T1 * 4), ("xT", DC * T1 * 2),
            ("xh", D * 4), ("xhT", DC * 3 * 2),
        ], 0)
        assert o1["fcw_hole"] == 16 * KB and o1["win"] == o2["cprojw"]
        x_s = [at(f"x_s{i}", [P, 2 * D], F32, o1[f"x{i}"]) for i in range(2)]
        win_s = at("win_s", [P, DC * 2 * I], BF16, o1["win"])
        convw_s = at("convw_s", [P, 64 * 256], BF16, o1["convw"])
        hbuf_s = at("hbuf_s", [P, IC * (T1 + 3)], BF16, o1["hbuf"])
        gate_s = at("gate_s", [P, IC * T1], F32, o1["gate"])
        u1_s = at("u1_s", [P, IC * T1], F32, o1["u1"])
        xT_s = at("xT_s", [P, DC * T1], BF16, o1["xT"])
        xh_s = at("xh_s", [P, D], F32, o1["xh"])
        xhT_s = at("xhT_s", [P, DC * 3], BF16, o1["xhT"])

        pb = [es.enter_context(nc.psum_tensor(f"pb{i}", [P, 512], F32))
              for i in range(8)]

        s_pe = b.sem("s_pe")
        s_vec = b.sem("s_vec")
        s_act = b.sem("s_act")
        s_gp = b.sem("s_gp")
        d_wA = b.sem("d_wA")
        d_wB = b.sem("d_wB")
        d_wC = b.sem("d_wC")
        d_wD = b.sem("d_wD")
        d_w1q = [b.sem(f"d_w1q{i}") for i in range(4)]
        d_w2q = [b.sem(f"d_w2q{i}") for i in range(4)]
        d_wb = b.sem("d_wb")
        d_sm = b.sem("d_sm")
        d_x = [b.sem("d_x0"), b.sem("d_x1")]
        d_xh = b.sem("d_xh")
        d_uo = b.sem("d_uo")
        d_ui = b.sem("d_ui")
        d_x2 = b.sem("d_x2")
        d_x1fo = b.sem("d_x1fo")
        d_xr = [b.sem("d_xr0"), b.sem("d_xr1")]
        d_sc = [b.sem("d_sc0"), b.sem("d_sc1")]
        d_gx = b.sem("d_gx")
        d_yo = [b.sem("d_yo0"), b.sem("d_yo1")]
        d_yg = b.sem("d_yg")
        d_xf = [b.sem("d_xf0"), b.sem("d_xf1")]
        d_out = [b.sem("d_out0"), b.sem("d_out1")]

        BNAMES = ["ident", "ones", "biases", "xh", "xhT", "win", "convw",
                  "fcw", "cprojw", "outw", "hbuf", "gate", "u1", "xT",
                  "x0", "x1", "u2", "ub", "t1", "hn", "mid", "sqh2", "x2",
                  "s", "x1f2", "x1cs", "w1", "w2", "h3", "gxf", "gxT",
                  "ysb_0", "ysb_1", "xrow_0", "xrow_1", "yt_0", "yt_1",
                  "x1ft_0", "x1ft_1", "sq3", "mask", "stats", "lnsm", "sm",
                  "b2r", "route", "zeros"]
        B = {n: _Buf() for n in BNAMES}
        B["x1cs"] = B["stats"]
        PB = [_Buf() for _ in range(8)]
        SP = {"u": _Buf(), "x1f": _Buf(), "x1b": _Buf(), "acc": _Buf()}

        def dma(q, out_ap, in_ap, waits, sem):
            return b.op(q, lambda e, o=out_ap, i=in_ap: e.dma_start(o, i),
                        waits=waits, inc=sem, inck=16)

        t = b.op("gpsimd", lambda e: e.memset(ident.ap(), 0.0), inc=s_gp)
        t = b.op("gpsimd", lambda e: e.affine_select(
            out=ident.ap(), in_=ident.ap(), compare_op=ALU.not_equal,
            fill=1.0, base=0, pattern=[[-1, P]], channel_multiplier=1),
            waits=[t], inc=s_gp)
        B["ident"].w = t
        b.op("gpsimd", lambda e: e.memset(ones_cb.ap(), 1.0), inc=s_gp)
        b.op("gpsimd", lambda e: e.memset(ones_rb.ap(), 1.0), inc=s_gp)
        t3 = b.op("gpsimd", lambda e: e.memset(ones_rf.ap(), 1.0), inc=s_gp)
        B["ones"].w = t3


        d_zf = b.sem("d_zf")
        for ap_d, ap_s in [(convb_d, convb_s), (lng_d, lng_s), (lnb_d, lnb_s),
                           (fcb_d, fcb_s), (cprojb_d, cprojb_s),
                           (b1t_d.rearrange("p e m -> p (e m)"), b1t_s),
                           (wg_d.rearrange("p d e -> p (d e)"), wg_s),
                           (bg_d, bg_s)]:
            dma("sync", ap_s.ap(), ap_d, [], d_sm)
        B["biases"].w = (d_sm, d_sm.n)

        for kc in range(DC):
            dma("gpsimd", win_s.ap()[:, kc * 2 * I:(kc + 1) * 2 * I],
                winT_d[kc * P:(kc + 1) * P, :], [], d_wA)
        for g in range(NG):
            src = convw_d[g].rearrange("k (icc p) o -> p (k icc) o", p=P)
            dst = convw_s.ap()[:, g * 8 * 256:(g + 1) * 8 * 256].rearrange(
                "p (ki o) -> p ki o", o=256)
            dma("gpsimd", dst, src, [], d_wA)
        n_ws_s1 = d_wA.n
        for q in range(4):
            src = fcwT_d[q * 4 * P:(q + 1) * 4 * P, :].rearrange(
                "(kc p) d -> p kc d", p=P)
            dma("gpsimd", fcw_s.ap()[:, q * 4 * D:(q + 1) * 4 * D].rearrange(
                "p (kc d) -> p kc d", d=D), src, [], d_wB)
        n_ws_fc = d_wB.n

        if 1 in stages:
            tz = b.op("gpsimd", lambda e: e.memset(xh_s.ap(), 0.0), inc=s_gp)
            tkh = dma("sync", xh_s.ap()[0:3, :], xh_d, [tz], d_xh)
            B["xh"].w = tkh
            for dc in range(DC):
                g = b.op("tensor", lambda e, dc=dc: e.transpose(
                    pb[6].ap()[:, 0:P], xh_s.ap()[:, dc * P:(dc + 1) * P],
                    ident.ap()),
                    waits=B["xh"].rd() + B["ident"].rd() + PB[6].war(),
                    inc=s_pe)
                PB[6].w = g
                B["xh"].r.append(g)
                ev = b.op("vector", lambda e, dc=dc: e.tensor_copy(
                    xhT_s.ap()[:, dc * 3:(dc + 1) * 3], pb[6].ap()[:, 0:3]),
                    waits=[g] + B["xhT"].war(), inc=s_vec)
                PB[6].r.append(ev)
                B["xhT"].w = ev
            for ch in range(IC):
                wts = B["xhT"].rd() + PB[7].war() + [(d_wA, n_ws_s1)]
                gtok = None
                for kc in range(DC):
                    gtok = b.op("tensor", lambda e, ch=ch, kc=kc: e.matmul(
                        pb[7].ap()[:, 0:3],
                        win_s.ap()[:, kc * 2 * I + ch * P: kc * 2 * I + (ch + 1) * P],
                        xhT_s.ap()[:, kc * 3:(kc + 1) * 3],
                        start=(kc == 0), stop=(kc == DC - 1)),
                        waits=(wts if kc == 0 else []),
                        inc=(s_pe if kc == DC - 1 else None))
                PB[7].w = gtok
                B["xhT"].r.append(gtok)
                B["win"].r.append(gtok)
                ev = b.op("scalar", lambda e, ch=ch: e.copy(
                    hbuf_s.ap()[:, ch * (T1 + 3): ch * (T1 + 3) + 3],
                    pb[7].ap()[:, 0:3]),
                    waits=[gtok] + B["hbuf"].war(), inc=s_act)
                PB[7].r.append(ev)
                B["hbuf"].w = ev

        st1_end = []
        if 1 in stages:
            xtoks = {}

            def emit_xload(ti):
                par = ti % 2
                xbuf = B[f"x{par}"]
                src = x_d[ti * T1:(ti + 1) * T1, :].rearrange(
                    "(tc p) d -> p tc d", p=P)
                tk = dma("sync", x_s[par].ap().rearrange("p (tc d) -> p tc d", d=D),
                         src, xbuf.war(), d_x[par])
                xbuf.wrote(tk)
                xtoks[ti] = tk

            emit_xload(0)
            for ti in range(NT1):
                par = ti % 2
                xs = x_s[par]
                xbuf = B[f"x{par}"]
                xtk = xtoks[ti]
                if ti > 0:
                    hv = b.op("vector", lambda e: e.tensor_copy(
                        hbuf_s.ap().rearrange("p (c j) -> p c j", j=T1 + 3)[:, :, 0:3],
                        hbuf_s.ap().rearrange("p (c j) -> p c j", j=T1 + 3)[:, :, T1:T1 + 3]),
                        waits=B["hbuf"].war(), inc=s_vec)
                    B["hbuf"].wrote(hv)
                for tc in range(2):
                    for dc in range(DC):
                        bank = 4 + ((tc * DC + dc) % 2)
                        g = b.op("tensor", lambda e, tc=tc, dc=dc, bank=bank, xs=xs: e.transpose(
                            pb[bank].ap()[:, 0:P],
                            xs.ap()[:, tc * D + dc * P: tc * D + (dc + 1) * P],
                            ident.ap()),
                            waits=[xtk] + B["ident"].rd() + PB[bank].war() +
                                  (B["xT"].war() if (tc == 0 and dc == 0) else []),
                            inc=s_pe)
                        PB[bank].w = g
                        xbuf.r.append(g)
                        if tc == 0 and dc == 0:
                            B["xT"].r = []
                        ev = b.op("vector", lambda e, tc=tc, dc=dc, bank=bank: e.tensor_copy(
                            xT_s.ap()[:, dc * T1 + tc * P: dc * T1 + (tc + 1) * P],
                            pb[bank].ap()[:, 0:P]),
                            waits=[g], inc=s_vec)
                        PB[bank].r.append(ev)
                        B["xT"].w = ev
                if ti + 1 < NT1:
                    emit_xload(ti + 1)
                for oc in range(2 * IC):
                    bank = oc % 2
                    w0 = B["xT"].rd() + PB[bank].war() + [(d_wA, n_ws_s1)]
                    g = None
                    for kc in range(DC):
                        g = b.op("tensor", lambda e, oc=oc, kc=kc, bank=bank: e.matmul(
                            pb[bank].ap()[:, 0:T1],
                            win_s.ap()[:, kc * 2 * I + oc * P: kc * 2 * I + (oc + 1) * P],
                            xT_s.ap()[:, kc * T1:(kc + 1) * T1],
                            start=(kc == 0), stop=(kc == DC - 1)),
                            waits=(w0 if kc == 0 else []),
                            inc=(s_pe if kc == DC - 1 else None))
                    PB[bank].w = g
                    B["xT"].r.append(g)
                    B["win"].r.append(g)
                    if oc < IC:
                        ev = b.op("scalar", lambda e, oc=oc, bank=bank: e.copy(
                            hbuf_s.ap()[:, oc * (T1 + 3) + 3: oc * (T1 + 3) + 3 + T1],
                            pb[bank].ap()[:, 0:T1]),
                            waits=[g] + B["hbuf"].rd(), inc=s_act)
                        B["hbuf"].r.append(ev)
                    else:
                        ev = b.op("scalar", lambda e, oc=oc, bank=bank: e.activation(
                            gate_s.ap()[:, (oc - IC) * T1:(oc - IC + 1) * T1],
                            pb[bank].ap()[:, 0:T1], AF.Silu),
                            waits=[g] + (B["gate"].war() if oc == IC else []),
                            inc=s_act)
                        if oc == IC:
                            B["gate"].r = []
                        B["gate"].w = ev
                    PB[bank].r.append(ev)
                hev_last = (s_act, s_act.n)
                for gi in range(NG):
                    for occ in range(2):
                        ch = gi * 2 + occ
                        bank = 2 + (ch % 2)
                        w0 = [hev_last] + PB[bank].war() + B["hbuf"].rd()
                        g = None
                        first = True
                        for k in range(KCV):
                            for icc in range(2):
                                last = (k == KCV - 1 and icc == 1)
                                g = b.op("tensor",
                                         lambda e, gi=gi, occ=occ, k=k, icc=icc, bank=bank, first=first: e.matmul(
                                             pb[bank].ap()[:, 0:T1],
                                             convw_s.ap()[:, ((gi * 4 + k) * 2 + icc) * 256 + occ * P:
                                                          ((gi * 4 + k) * 2 + icc) * 256 + (occ + 1) * P],
                                             hbuf_s.ap()[:, (gi * 2 + icc) * (T1 + 3) + k:
                                                         (gi * 2 + icc) * (T1 + 3) + k + T1],
                                             start=first, stop=(k == KCV - 1 and icc == 1)),
                                         waits=(w0 if first else []),
                                         inc=(s_pe if last else None))
                                first = False
                        PB[bank].w = g
                        B["hbuf"].r.append(g)
                        B["convw"].r.append(g)
                        ev = b.op("vector", lambda e, ch=ch, bank=bank: e.scalar_tensor_tensor(
                            u1_s.ap()[:, ch * T1:(ch + 1) * T1],
                            pb[bank].ap()[:, 0:T1],
                            convb_s.ap()[:, ch:ch + 1],
                            gate_s.ap()[:, ch * T1:(ch + 1) * T1],
                            ALU.add, ALU.mult),
                            waits=[g, B["gate"].w, B["biases"].w] +
                                  (B["u1"].war() if ch == 0 else []),
                            inc=s_vec)
                        PB[bank].r.append(ev)
                        B["gate"].r.append(ev)
                        if ch == 0:
                            B["u1"].r = []
                        B["u1"].w = ev
                dstu = u_sp.rearrange("(c p) t -> p c t", p=P)[:, :, ti * T1:(ti + 1) * T1]
                tko = dma("sync", dstu,
                          u1_s.ap().rearrange("p (c t) -> p c t", t=T1),
                          [B["u1"].w], d_uo)
                B["u1"].r.append(tko)
                SP["u"].w = tko
            st1_end = [(s_pe, s_pe.n), (s_act, s_act.n), (s_vec, s_vec.n),
                       (d_uo, d_uo.n)]
            for n in ["fcw", "cprojw", "outw", "u2", "ub", "t1", "hn", "mid",
                      "sqh2", "x2", "s", "x1f2", "x1cs"]:
                B[n].r = list(st1_end)

        if 2 in stages:
            for q in range(2):
                src = cprojwT_d[q * 4 * P:(q + 1) * 4 * P, :].rearrange(
                    "(kc p) d -> p kc d", p=P)
                dma("gpsimd", cprojw_s.ap()[:, q * 4 * I:(q + 1) * 4 * I].rearrange(
                    "p (kc d) -> p kc d", d=I), src, st1_end, d_wC)
            for q in range(4):
                src = outwT_d[q * 4 * P:(q + 1) * 4 * P, :].rearrange(
                    "(kc p) d -> p kc d", p=P)
                dma("gpsimd", outw_s.ap()[:, q * 4 * D:(q + 1) * 4 * D].rearrange(
                    "p (kc d) -> p kc d", d=D), src, st1_end, d_wD)
        n_ws_cp = d_wC.n
        n_ws_all = d_wD.n
        if 3 in stages:
            src = w1_d[0].rearrange("(kc p) m -> p kc m", p=P)[:, :, 0:MMLP // 4]
            dma("gpsimd", w1_s.ap().rearrange(
                "p (q kc m) -> p q kc m", q=4, kc=DC)[:, 0],
                src, st1_end, d_w1q[0])

        st2_end = []
        if 2 in stages:
            utoks = {}
            x2toks = {}

            def emit_uload(ti):
                srcu = u_sp.rearrange("(c p) t -> p c t", p=P)[:, :, ti * T1:(ti + 1) * T1]
                tk = dma("sync", u2_s.ap().rearrange("p (c t) -> p c t", t=T1),
                         srcu, B["u2"].war() + SP["u"].rd(), d_ui)
                B["u2"].wrote(tk)
                utoks[ti] = tk

            def emit_x2load(ti):
                xsrc = x_d[ti * T1:(ti + 1) * T1, :].rearrange("(tc p) d -> p tc d", p=P)
                tk = dma("sync", x2_s.ap().rearrange("p (tc d) -> p tc d", d=D),
                         xsrc, B["x2"].war(), d_x2)
                B["x2"].wrote(tk)
                x2toks[ti] = tk

            ubtoks = {}
            sqtoks = {}

            def emit_ubsq(ti):
                tk = utoks[ti]
                ubt = b.op("scalar", lambda e: e.copy(ub_s.ap(), u2_s.ap()),
                           waits=[tk] + B["ub"].war(), inc=s_act)
                B["ub"].wrote(ubt)
                B["u2"].r.append(ubt)
                sqt = b.op("vector", lambda e: e.tensor_tensor(
                    sq2_s.ap(), u2_s.ap(), u2_s.ap(), ALU.mult),
                    waits=[tk] + B["sqh2"].war(), inc=s_vec)
                B["sqh2"].wrote(sqt)
                B["u2"].r.append(sqt)
                ubtoks[ti] = ubt
                sqtoks[ti] = sqt

            emit_uload(0)
            emit_x2load(0)
            emit_ubsq(0)
            s_par = [s_s, x1f2_s]
            B_s = [B["s"], B["x1f2"]]

            def make_tail(ti, x1tok):
                par = ti % 2
                sp = s_par[par]
                sb = B_s[par]

                def tail():
                    dstf = x1f_sp[ti * T1:(ti + 1) * T1, :].rearrange(
                        "(tc p) d -> p tc d", p=P)
                    tkfo = dma("sync", dstf,
                               sp.ap().rearrange("p (tc d) -> p tc d", d=D),
                               [x1tok], d_x1fo)
                    sb.r.append(tkfo)
                    SP["x1f"].w = tkfo
                    if 3 in stages:
                        for rep in range(NE * CAP // t_core):
                            dma("scalar",
                                gathx_sp[rep * t_core + ti * T1:
                                         rep * t_core + (ti + 1) * T1, :],
                                x1f_sp[ti * T1:(ti + 1) * T1, :],
                                [tkfo], d_zf)
                    lg_banks = [2, 3]
                    for dc in range(DC):
                        evs = []
                        for tc in range(2):
                            bank = (dc * 2 + tc) % 2
                            g = b.op("tensor", lambda e, tc=tc, dc=dc, bank=bank, sp=sp: e.transpose(
                                pb[bank].ap()[:, 0:P],
                                sp.ap()[:, tc * D + dc * P: tc * D + (dc + 1) * P],
                                ident.ap()),
                                waits=[x1tok] + PB[bank].war(), inc=s_pe)
                            PB[bank].w = g
                            sb.r.append(g)
                            ev = b.op("vector", lambda e, tc=tc, bank=bank: e.tensor_copy(
                                x1cs_s.ap()[:, tc * P:(tc + 1) * P],
                                pb[bank].ap()[:, 0:P]),
                                waits=[g] + B["x1cs"].war(), inc=s_vec)
                            B["x1cs"].wrote(ev)
                            PB[bank].r.append(ev)
                            evs.append(ev)
                        for tc in range(2):
                            bank = lg_banks[tc]
                            g = b.op("tensor", lambda e, tc=tc, dc=dc, bank=bank: e.matmul(
                                pb[bank].ap()[:, 256:256 + NE],
                                x1cs_s.ap()[:, tc * P:(tc + 1) * P],
                                wg_s.ap()[:, dc * NE:(dc + 1) * NE],
                                start=(dc == 0), stop=False),
                                waits=[evs[tc], B["biases"].w] +
                                      (PB[bank].war() if dc == 0 else []),
                                inc=s_pe)
                            PB[bank].w = g
                            B["x1cs"].r.append(g)
                    for tc in range(2):
                        bank = lg_banks[tc]
                        g = b.op("tensor", lambda e, bank=bank: e.matmul(
                            pb[bank].ap()[:, 256:256 + NE], ones_rf.ap(), bg_s.ap(),
                            start=False, stop=True),
                            waits=[B["ones"].w], inc=s_pe)
                        PB[bank].w = g
                    for tc in range(2):
                        bank = lg_banks[tc]
                        lg = PB[bank].w
                        t_mx = b.op("vector", lambda e, bank=bank: e.reduce_max(
                            sm_s.ap()[:, 16:17], pb[bank].ap()[:, 256:256 + NE],
                            axis=AX.X), waits=[lg] + B["sm"].war(), inc=s_vec)
                        B["sm"].r = []
                        t_sh = b.op("vector", lambda e, bank=bank: e.tensor_scalar(
                            sm_s.ap()[:, 0:NE], pb[bank].ap()[:, 256:256 + NE],
                            sm_s.ap()[:, 16:17], None, ALU.subtract),
                            waits=[t_mx], inc=s_vec)
                        PB[bank].r += [t_mx, t_sh]
                        t_ex = b.op("scalar", lambda e: e.activation(
                            sm_s.ap()[:, 8:8 + NE], sm_s.ap()[:, 0:NE], AF.Exp),
                            waits=[t_sh], inc=s_act)
                        t_sm = b.op("vector", lambda e: e.reduce_sum(
                            sm_s.ap()[:, 17:18], sm_s.ap()[:, 8:8 + NE], axis=AX.X),
                            waits=[t_ex], inc=s_vec)
                        t_rp = b.op("vector", lambda e: e.reciprocal(
                            sm_s.ap()[:, 18:19], sm_s.ap()[:, 17:18]),
                            waits=[t_sm], inc=s_vec)
                        t_im = b.op("vector", lambda e: e.tensor_scalar(
                            sm_s.ap()[:, 0:NE], sm_s.ap()[:, 0:NE], 0.0, None,
                            ALU.is_equal), waits=[t_sh, t_ex, t_rp], inc=s_vec)
                        t_mk = b.op("vector", lambda e: e.tensor_scalar(
                            sm_s.ap()[:, 0:NE], sm_s.ap()[:, 0:NE],
                            sm_s.ap()[:, 18:19], None, ALU.mult),
                            waits=[t_im], inc=s_vec)
                        c = 2 * ti + tc
                        t_wr = b.op("vector", lambda e, c=c: e.tensor_copy(
                            mask_s.ap().rearrange("p (e c) -> p e c", c=NCH)[:, :, c:c + 1],
                            sm_s.ap()[:, 0:NE].rearrange("p (e o) -> p e o", o=1)),
                            waits=[t_mk] + B["mask"].rd(), inc=s_vec)
                        B["sm"].w = t_wr
                        B["mask"].w = t_wr
                return tail

            pend = None
            for ti in range(NT1):
                par = ti % 2
                sp = s_par[par]
                sb = B_s[par]
                utk = utoks[ti]
                xtk = x2toks[ti]
                ubt = ubtoks[ti]
                sqt = sqtoks[ti]
                g1 = None
                for ch in range(IC):
                    g1 = b.op("tensor", lambda e, ch=ch: e.matmul(
                        pb[4].ap()[0:1, 0:T1], ones_cb.ap(),
                        ub_s.ap()[:, ch * T1:(ch + 1) * T1],
                        start=(ch == 0), stop=(ch == IC - 1)),
                        waits=([ubt, B["ones"].w] + PB[4].war()) if ch == 0 else [],
                        inc=(s_pe if ch == IC - 1 else None))
                B["ub"].r.append(g1)
                g2 = None
                for ch in range(IC):
                    g2 = b.op("tensor", lambda e, ch=ch: e.matmul(
                        pb[4].ap()[0:1, T1:2 * T1], ones_cb.ap(),
                        sq2_s.ap()[:, ch * T1:(ch + 1) * T1],
                        start=(ch == 0), stop=(ch == IC - 1)),
                        waits=[sqt] if ch == 0 else [],
                        inc=(s_pe if ch == IC - 1 else None))
                PB[4].w = g2
                B["sqh2"].r.append(g2)
                if pend is not None:
                    pend()
                tmu = b.op("vector", lambda e: e.tensor_scalar(
                    stats_s.ap()[:, 0:T1], pb[4].ap()[0:1, 0:T1],
                    1.0 / I, None, ALU.mult),
                    waits=[g1, g2] + B["stats"].war(), inc=s_vec)
                tms = b.op("vector", lambda e: e.tensor_scalar(
                    stats_s.ap()[:, T1:2 * T1], pb[4].ap()[0:1, T1:2 * T1],
                    1.0 / I, None, ALU.mult), waits=[], inc=s_vec)
                PB[4].r += [tmu, tms]
                tm2 = b.op("vector", lambda e: e.tensor_tensor(
                    stats_s.ap()[:, 2 * T1:3 * T1], stats_s.ap()[:, 0:T1],
                    stats_s.ap()[:, 0:T1], ALU.mult), waits=[tmu], inc=s_vec)
                tvar = b.op("vector", lambda e: e.scalar_tensor_tensor(
                    stats_s.ap()[:, 2 * T1:3 * T1], stats_s.ap()[:, T1:2 * T1],
                    EPS, stats_s.ap()[:, 2 * T1:3 * T1], ALU.add, ALU.subtract),
                    waits=[tms, tm2], inc=s_vec)
                tln = b.op("scalar", lambda e: e.activation(
                    stats_s.ap()[:, 3 * T1:4 * T1], stats_s.ap()[:, 2 * T1:3 * T1],
                    AF.Ln), waits=[tvar], inc=s_act)
                trs = b.op("scalar", lambda e: e.activation(
                    stats_s.ap()[:, 3 * T1:4 * T1], stats_s.ap()[:, 3 * T1:4 * T1],
                    AF.Exp, scale=-0.5), waits=[tln], inc=s_act)
                B["stats"].wrote(trs)
                gb1 = b.op("tensor", lambda e: e.matmul(
                    pb[5].ap()[:, 0:T1], ones_rf.ap(), stats_s.ap()[:, 0:T1],
                    start=True, stop=True),
                    waits=[tmu, trs, B["ones"].w] + PB[5].war(), inc=s_pe)
                gb2 = b.op("tensor", lambda e: e.matmul(
                    pb[5].ap()[:, T1:2 * T1], ones_rf.ap(),
                    stats_s.ap()[:, 3 * T1:4 * T1],
                    start=True, stop=True), waits=[], inc=s_pe)
                PB[5].w = gb2
                B["stats"].r += [gb1, gb2]
                last_th = None
                for ch in range(IC):
                    ta = b.op("vector", lambda e, ch=ch: e.tensor_tensor(
                        t1_s.ap()[:, ch * T1:(ch + 1) * T1],
                        u2_s.ap()[:, ch * T1:(ch + 1) * T1],
                        pb[5].ap()[:, 0:T1], ALU.subtract),
                        waits=[gb1, gb2, utk] + (B["t1"].war() if ch == 0 else []),
                        inc=s_vec)
                    if ch == 0:
                        B["t1"].r = []
                    tb2 = b.op("vector", lambda e, ch=ch: e.tensor_tensor(
                        t1_s.ap()[:, ch * T1:(ch + 1) * T1],
                        t1_s.ap()[:, ch * T1:(ch + 1) * T1],
                        pb[5].ap()[:, T1:2 * T1], ALU.mult),
                        waits=[ta], inc=s_vec)
                    PB[5].r += [ta, tb2]
                    B["t1"].w = tb2
                    B["u2"].r.append(ta)
                    th = b.op("scalar", lambda e, ch=ch: e.activation(
                        hn_s.ap()[:, ch * T1:(ch + 1) * T1],
                        t1_s.ap()[:, ch * T1:(ch + 1) * T1],
                        AF.Identity, bias=lnb_s.ap()[:, ch:ch + 1],
                        scale=lng_s.ap()[:, ch:ch + 1]),
                        waits=[tb2, B["biases"].w] + (B["hn"].war() if ch == 0 else []),
                        inc=s_act)
                    if ch == 0:
                        B["hn"].r = []
                    B["hn"].w = th
                    B["t1"].r.append(th)
                    last_th = th
                hn_all = (s_act, s_act.n)
                for mcc in range(DC):
                    bank = mcc % 2
                    g = None
                    for kc in range(IC):
                        g = b.op("tensor", lambda e, mcc=mcc, kc=kc, bank=bank: e.matmul(
                            pb[bank].ap()[:, 0:T1],
                            fcw_s.ap()[:, kc * D + mcc * P: kc * D + (mcc + 1) * P],
                            hn_s.ap()[:, kc * T1:(kc + 1) * T1],
                            start=(kc == 0), stop=(kc == IC - 1)),
                            waits=([hn_all, (d_wB, n_ws_fc)] + PB[bank].war())
                            if kc == 0 else [],
                            inc=(s_pe if kc == IC - 1 else None))
                    PB[bank].w = g
                    B["hn"].r.append(g)
                    B["fcw"].r.append(g)
                    ev = b.op("scalar", lambda e, mcc=mcc, bank=bank: e.activation(
                        mid_s.ap()[:, mcc * T1:(mcc + 1) * T1],
                        pb[bank].ap()[:, 0:T1], AF.Silu,
                        bias=fcb_s.ap()[:, mcc:mcc + 1]),
                        waits=[g, B["biases"].w] + (B["mid"].war() if mcc == 0 else []),
                        inc=s_act)
                    PB[bank].r.append(ev)
                    if mcc == 0:
                        B["mid"].r = []
                    B["mid"].w = ev
                mid_all = (s_act, s_act.n)
                for ch in range(IC):
                    bank = 2 + ch % 2
                    g = None
                    for kc in range(DC):
                        g = b.op("tensor", lambda e, ch=ch, kc=kc, bank=bank: e.matmul(
                            pb[bank].ap()[:, 0:T1],
                            cprojw_s.ap()[:, kc * I + ch * P: kc * I + (ch + 1) * P],
                            mid_s.ap()[:, kc * T1:(kc + 1) * T1],
                            start=(kc == 0), stop=(kc == DC - 1)),
                            waits=([mid_all, (d_wC, n_ws_cp)] + PB[bank].war())
                            if kc == 0 else [],
                            inc=(s_pe if kc == DC - 1 else None))
                    PB[bank].w = g
                    B["mid"].r.append(g)
                    B["cprojw"].r.append(g)
                    ev = b.op("vector", lambda e, ch=ch, bank=bank: e.scalar_tensor_tensor(
                        h2_s.ap()[:, ch * T1:(ch + 1) * T1],
                        pb[bank].ap()[:, 0:T1],
                        cprojb_s.ap()[:, ch:ch + 1],
                        u2_s.ap()[:, ch * T1:(ch + 1) * T1],
                        ALU.add, ALU.add),
                        waits=[g, B["biases"].w, utk] +
                              (B["sqh2"].war() if ch == 0 else []),
                        inc=s_vec)
                    PB[bank].r.append(ev)
                    B["u2"].r.append(ev)
                    if ch == 0:
                        B["sqh2"].r = []
                    B["sqh2"].w = ev
                h2_all = (s_vec, s_vec.n)
                if ti + 1 < NT1:
                    emit_uload(ti + 1)
                for tc in range(2):
                    for dh in range(2):
                        bank = 4 + tc * 2 + dh
                        g = None
                        for kc in range(IC):
                            g = b.op("tensor", lambda e, tc=tc, dh=dh, kc=kc, bank=bank: e.matmul(
                                pb[bank].ap()[:, 0:512],
                                h2_s.ap()[:, kc * T1 + tc * P: kc * T1 + (tc + 1) * P],
                                outw_s.ap()[:, kc * D + dh * 512: kc * D + (dh + 1) * 512],
                                start=(kc == 0), stop=(kc == IC - 1)),
                                waits=([h2_all, (d_wD, n_ws_all)] + PB[bank].war())
                                if kc == 0 else [],
                                inc=(s_pe if kc == IC - 1 else None))
                        PB[bank].w = g
                        B["sqh2"].r.append(g)
                        B["outw"].r.append(g)
                for tc in range(2):
                    stoks = []
                    for dh in range(2):
                        bank = 4 + tc * 2 + dh
                        tv = b.op("vector", lambda e, tc=tc, dh=dh, bank=bank, sp=sp: e.tensor_tensor(
                            sp.ap()[:, tc * D + dh * 512: tc * D + (dh + 1) * 512],
                            x2_s.ap()[:, tc * D + dh * 512: tc * D + (dh + 1) * 512],
                            pb[bank].ap()[:, 0:512], ALU.add),
                            waits=[PB[bank].w, xtk] +
                                  (sb.war() if (tc == 0 and dh == 0) else []),
                            inc=s_vec)
                        PB[bank].r.append(tv)
                        B["x2"].r.append(tv)
                        if tc == 0 and dh == 0:
                            sb.r = []
                        sb.w = tv
                        stoks.append(tv)
                    c0 = 8 * tc
                    tsq = b.op("vector", lambda e, tc=tc, sp=sp: e.tensor_tensor(
                        sqt2_s.ap(), sp.ap()[:, tc * D:(tc + 1) * D],
                        sp.ap()[:, tc * D:(tc + 1) * D], ALU.mult),
                        waits=stoks + B["t1"].war(), inc=s_vec)
                    B["t1"].wrote(tsq)
                    tr1 = b.op("vector", lambda e, tc=tc, c0=c0, sp=sp: e.reduce_sum(
                        lnsm_s.ap()[:, c0:c0 + 1], sp.ap()[:, tc * D:(tc + 1) * D],
                        axis=AX.X), waits=stoks + B["lnsm"].war(), inc=s_vec)
                    if True:
                        B["lnsm"].r = []
                    tr2 = b.op("vector", lambda e, c0=c0: e.reduce_sum(
                        lnsm_s.ap()[:, c0 + 1:c0 + 2], sqt2_s.ap(),
                        axis=AX.X), waits=[tsq], inc=s_vec)
                    B["t1"].r.append(tr2)
                    tmu2 = b.op("vector", lambda e, c0=c0: e.tensor_scalar(
                        lnsm_s.ap()[:, c0 + 2:c0 + 3], lnsm_s.ap()[:, c0:c0 + 1],
                        1.0 / D, None, ALU.mult), waits=[tr1], inc=s_vec)
                    tms2 = b.op("vector", lambda e, c0=c0: e.tensor_scalar(
                        lnsm_s.ap()[:, c0 + 3:c0 + 4], lnsm_s.ap()[:, c0 + 1:c0 + 2],
                        1.0 / D, None, ALU.mult), waits=[tr2], inc=s_vec)
                    tm22 = b.op("vector", lambda e, c0=c0: e.tensor_tensor(
                        lnsm_s.ap()[:, c0 + 4:c0 + 5], lnsm_s.ap()[:, c0 + 2:c0 + 3],
                        lnsm_s.ap()[:, c0 + 2:c0 + 3], ALU.mult),
                        waits=[tmu2], inc=s_vec)
                    tvr2 = b.op("vector", lambda e, c0=c0: e.scalar_tensor_tensor(
                        lnsm_s.ap()[:, c0 + 5:c0 + 6], lnsm_s.ap()[:, c0 + 3:c0 + 4],
                        EPS, lnsm_s.ap()[:, c0 + 4:c0 + 5], ALU.add, ALU.subtract),
                        waits=[tms2, tm22], inc=s_vec)
                    tl2 = b.op("scalar", lambda e, c0=c0: e.activation(
                        lnsm_s.ap()[:, c0 + 6:c0 + 7], lnsm_s.ap()[:, c0 + 5:c0 + 6],
                        AF.Ln), waits=[tvr2], inc=s_act)
                    tr2b = b.op("scalar", lambda e, c0=c0: e.activation(
                        lnsm_s.ap()[:, c0 + 7:c0 + 8], lnsm_s.ap()[:, c0 + 6:c0 + 7],
                        AF.Exp, scale=-0.5), waits=[tl2], inc=s_act)
                    B["lnsm"].w = tr2b
                    tx1 = b.op("vector", lambda e, tc=tc, c0=c0, sp=sp: e.tensor_scalar(
                        sp.ap()[:, tc * D:(tc + 1) * D],
                        sp.ap()[:, tc * D:(tc + 1) * D],
                        lnsm_s.ap()[:, c0 + 2:c0 + 3],
                        lnsm_s.ap()[:, c0 + 7:c0 + 8],
                        ALU.subtract, ALU.mult),
                        waits=[tr2b], inc=s_vec)
                    B["lnsm"].r.append(tx1)
                    sb.w = tx1
                x1tok = sb.w
                if ti + 1 < NT1:
                    emit_x2load(ti + 1)
                    emit_ubsq(ti + 1)
                pend = make_tail(ti, x1tok)
            pend()
            st2_end = [(s_pe, s_pe.n), (s_act, s_act.n), (s_vec, s_vec.n),
                       (d_x1fo, d_x1fo.n)]
            for n in ["w1", "w2", "h3", "gxf", "gxT", "ysb_0", "ysb_1",
                      "xrow_0", "xrow_1", "sq3"]:
                B[n].r = list(st2_end)

        if 3 in stages:
            TPE = CAP // T3
            NT3S = NE * TPE
            MQ = MMLP // 4

            def w1_src(e_i, q):
                return w1_d[e_i].rearrange("(kc p) m -> p kc m", p=P)[
                    :, :, q * MQ:(q + 1) * MQ]

            def w1_dst(q):
                return w1_s.ap().rearrange("p (q kc m) -> p q kc m", q=4, kc=DC)[:, q]

            def w2_src(e_i, dh, half):
                return w2_d[e_i].rearrange("(mc p) d -> p mc d", p=P)[
                    :, half * 16:(half + 1) * 16, dh * 512:(dh + 1) * 512]

            def w2_dst(dh, half):
                return w2_s.ap()[:, dh * MC * 512 + half * 16 * 512:
                                 dh * MC * 512 + (half + 1) * 16 * 512].rearrange(
                    "p (mc j) -> p mc j", j=512)

            dma("gpsimd", b2r_s.ap(), b2_d[0:1, :], st2_end + B["b2r"].war(), d_wb)
            for q in range(1, 4):
                dma("gpsimd", w1_dst(q), w1_src(0, q), st2_end, d_w1q[q])
            for dh in range(2):
                for half in range(2):
                    dma("gpsimd", w2_dst(dh, half), w2_src(0, dh, half),
                        st2_end, d_w2q[dh * 2 + half])

            rt = []

            def vop(make, waits, eng="vector", inc=None):
                tok = b.op(eng, make, waits=waits, inc=inc or s_vec)
                rt.append(tok)
                return tok

            mtok = B["mask"].w
            st25 = [(s_pe, s_pe.n), (s_act, s_act.n), (s_vec, s_vec.n)]
            t_z = vop(lambda e: e.memset(zeros_s.ap(), 0.0), st25)
            B["zeros"].w = t_z
            t_oh = vop(lambda e: e.tensor_scalar(
                oh_s.ap(), mask_s.ap(), 0.0, None, ALU.is_gt), [mtok] + st25)
            t_p = vop(lambda e: e.tensor_reduce(
                p_tip.ap(),
                mask_s.ap().rearrange("p (e c) -> p c e", e=NE),
                axis=AX.X, op=ALU.add), [mtok])
            B["route"].w = t_p
            scans = []
            for e_i in range(NE):
                tk = vop(lambda e, e_i=e_i: e.tensor_tensor_scan(
                    scn_s.ap()[:, e_i * NCH:(e_i + 1) * NCH],
                    oh_s.ap()[:, e_i * NCH:(e_i + 1) * NCH],
                    zeros_s.ap()[:, 0:NCH], 0.0, ALU.add, ALU.add),
                    [t_oh, B["zeros"].w])
                scans.append(tk)
            t_rows = vop(lambda e: e.tensor_copy(
                rows_s.ap(),
                scn_s.ap().rearrange("p (e c) -> p e c", e=NE)[:, :, NCH - 1:NCH]
                .rearrange("p e o -> p (e o)")), [scans[-1]])
            t_sx = vop(lambda e: e.tensor_tensor(
                scn_s.ap(), scn_s.ap(), oh_s.ap(), ALU.subtract), [t_rows])
            g1 = b.op("tensor", lambda e: e.transpose(
                pb[0].ap()[0:NE, 0:P], rows_s.ap(), ident.ap()),
                waits=[t_rows, B["ident"].w] + PB[0].war(), inc=s_pe)
            PB[0].w = g1
            t_rT = vop(lambda e: e.tensor_copy(rT_s.ap(), pb[0].ap()[0:NE, 0:P]),
                       [g1])
            PB[0].r.append(t_rT)
            t_ri = vop(lambda e: e.tensor_tensor_scan(
                rTi_s.ap(), rT_s.ap(), zeros_s.ap()[0:NE, 0:P],
                0.0, ALU.add, ALU.add), [t_rT])
            t_rx = vop(lambda e: e.tensor_tensor(
                rTi_s.ap(), rTi_s.ap(), rT_s.ap(), ALU.subtract), [t_ri])
            g2 = b.op("tensor", lambda e: e.transpose(
                pb[0].ap()[:, 0:NE], rTi_s.ap(), ident.ap()[0:NE, 0:NE]),
                waits=[t_rx] + PB[0].war(), inc=s_pe)
            PB[0].w = g2
            t_po = vop(lambda e: e.tensor_copy(poff_s.ap(), pb[0].ap()[:, 0:NE]),
                       [g2])
            PB[0].r.append(t_po)
            t_prev = t_po
            for e_i in range(NE):
                t_a = vop(lambda e, e_i=e_i: e.tensor_scalar(
                    scn_s.ap()[:, e_i * NCH:(e_i + 1) * NCH],
                    scn_s.ap()[:, e_i * NCH:(e_i + 1) * NCH],
                    poff_s.ap()[:, e_i:e_i + 1], float(e_i * CAP),
                    ALU.add, ALU.add), [t_po, t_sx, t_prev])
                t_b = vop(lambda e, e_i=e_i: e.tensor_scalar(
                    tmpc_s.ap(), scn_s.ap()[:, e_i * NCH:(e_i + 1) * NCH],
                    float((e_i + 1) * CAP), None, ALU.is_ge), [t_a])
                t_c = vop(lambda e, e_i=e_i: e.scalar_tensor_tensor(
                    scn_s.ap()[:, e_i * NCH:(e_i + 1) * NCH],
                    tmpc_s.ap(), 1.0e9,
                    scn_s.ap()[:, e_i * NCH:(e_i + 1) * NCH],
                    ALU.mult, ALU.add), [t_b])
                t_prev = t_c
            t_d0 = vop(lambda e: e.memset(dest_s.ap(), 0.0), [t_prev])
            t_acc = t_d0
            for e_i in range(NE):
                t_m = vop(lambda e, e_i=e_i: e.tensor_tensor(
                    tmpc_s.ap(), oh_s.ap()[:, e_i * NCH:(e_i + 1) * NCH],
                    scn_s.ap()[:, e_i * NCH:(e_i + 1) * NCH], ALU.mult),
                    [t_acc])
                t_acc = vop(lambda e: e.tensor_tensor(
                    dest_s.ap(), dest_s.ap(), tmpc_s.ap(), ALU.add), [t_m])
            t_di = vop(lambda e: e.tensor_copy(dest_i.ap(), dest_s.ap()), [t_acc])

            bnd_reg = nc.gpsimd.alloc_register("bnd")
            b.op("gpsimd", lambda e: e.reg_mov(bnd_reg, NSLOT - 1), waits=[])
            for c in range(NCH):
                par = c % 2
                xr = xrow_s[par]
                xrb = B[f"xrow_{par}"]
                tk = dma("sync", xr.ap(),
                         x1f_sp[c * P:(c + 1) * P, :],
                         xrb.war() + SP["x1f"].rd(), d_xr[par])
                xrb.wrote(tk)
                sc = b.op("gpsimd", lambda e, c=c, xr=xr: e.indirect_dma_start(
                    gathx_sp, IndirectOffsetOnAxis(dest_i.ap()[:, c:c + 1], 0),
                    xr.ap(), None,
                    bounds_check=bnd_reg, oob_is_err=False),
                    waits=[tk, t_di, (d_zf, d_zf.n)], inc=d_sc[par], inck=16)
                xrb.r.append(sc)
            sc_all = [(d_sc[0], d_sc[0].n), (d_sc[1], d_sc[1].n)]


            gxtoks = {}

            def emit_gxload(k):
                tk = dma("sync", gxf_s.ap().rearrange("p (tc d) -> p tc d", d=D),
                         gathx_sp[k * T3:(k + 1) * T3, :].rearrange(
                             "(tc p) d -> p tc d", p=P),
                         B["gxf"].war() + sc_all, d_gx)
                B["gxf"].wrote(tk)
                gxtoks[k] = tk

            emit_gxload(0)
            ysb_par = 0
            for e_i in range(NE):
                w1_need = {q: (d_w1q[q], 16 * (e_i + 1)) for q in range(4)}
                w2_need = {dh: [(d_w2q[dh * 2], 16 * (e_i + 1)),
                                (d_w2q[dh * 2 + 1], 16 * (e_i + 1))]
                           for dh in range(2)}
                b2_need = [(d_wb, 16 * (e_i + 1))]
                w1_q_last = {}
                w2_dh_last = {}
                b2_readers = []
                b2b = {}
                for dh in range(2):
                    gb = b.op("tensor", lambda e, dh=dh: e.matmul(
                        pb[6].ap()[:, 0:512], ones_rb.ap(),
                        b2r_s.ap()[0:1, dh * 512:(dh + 1) * 512],
                        start=True, stop=True),
                        waits=[B["ones"].w] + b2_need + PB[6].war(),
                        inc=s_pe)
                    PB[6].w = gb
                    b2_readers.append(gb)
                    evb = b.op("vector", lambda e, dh=dh: e.tensor_copy(
                        b2b_s.ap()[:, dh * 512:(dh + 1) * 512],
                        pb[6].ap()[:, 0:512]),
                        waits=[gb] + (B["b2r"].war() if dh == 0 else []),
                        inc=s_vec)
                    PB[6].r.append(evb)
                    b2b[dh] = evb
                for tpe_i in range(TPE):
                    k = e_i * TPE + tpe_i
                    gxtk = gxtoks[k]
                    for tc in range(4):
                        for dc in range(DC):
                            bank = 6 + ((tc * DC + dc) % 2)
                            g = b.op("tensor", lambda e, tc=tc, dc=dc, bank=bank: e.transpose(
                                pb[bank].ap()[:, 0:P],
                                gxf_s.ap()[:, tc * D + dc * P: tc * D + (dc + 1) * P],
                                ident.ap()),
                                waits=[gxtk, B["ident"].w] + PB[bank].war() +
                                      (B["gxT"].war() if (tc == 0 and dc == 0) else []),
                                inc=s_pe)
                            PB[bank].w = g
                            B["gxf"].r.append(g)
                            if tc == 0 and dc == 0:
                                B["gxT"].r = []
                            ev = b.op("vector", lambda e, tc=tc, dc=dc, bank=bank: e.tensor_copy(
                                gxT_s.ap()[:, dc * T3 + tc * P: dc * T3 + (tc + 1) * P],
                                pb[bank].ap()[:, 0:P]),
                                waits=[g], inc=s_vec)
                            PB[bank].r.append(ev)
                            B["gxT"].w = ev
                    gxT_all = (s_vec, s_vec.n)
                    H_BANKS = [0, 1, 6, 7]
                    for mcc in range(MC):
                        q, j = mcc // 8, mcc % 8
                        bank = H_BANKS[mcc % 4]
                        pbw = PB[bank].war()
                        g = None
                        for kc in range(DC):
                            g = b.op("tensor", lambda e, q=q, j=j, kc=kc, bank=bank: e.matmul(
                                pb[bank].ap()[:, 0:T3],
                                w1_s.ap()[:, (q * DC + kc) * MQ + j * P:
                                          (q * DC + kc) * MQ + (j + 1) * P],
                                gxT_s.ap()[:, kc * T3:(kc + 1) * T3],
                                start=(kc == 0), stop=(kc == DC - 1)),
                                waits=([gxT_all, w1_need[q]] + pbw +
                                       (B["h3"].war() if mcc == 0 else []))
                                if kc == 0 else [],
                                inc=(s_pe if kc == DC - 1 else None))
                        PB[bank].w = g
                        B["gxT"].r.append(g)
                        w1_q_last[q] = g
                        if mcc == 0:
                            B["h3"].r = []
                        if mcc % 2 == 0:
                            ev = b.op("scalar", lambda e, mcc=mcc, bank=bank, e_i=e_i: e.activation(
                                h3_s.ap()[:, mcc * T3:(mcc + 1) * T3],
                                pb[bank].ap()[:, 0:T3], AF.Relu,
                                bias=b1t_s.ap()[:, e_i * MC + mcc:e_i * MC + mcc + 1]),
                                waits=[g, B["biases"].w], inc=s_act)
                        else:
                            ev = b.op("vector", lambda e, mcc=mcc, bank=bank, e_i=e_i: e.tensor_scalar(
                                h3_s.ap()[:, mcc * T3:(mcc + 1) * T3],
                                pb[bank].ap()[:, 0:T3],
                                b1t_s.ap()[:, e_i * MC + mcc:e_i * MC + mcc + 1],
                                0.0, ALU.add, ALU.max),
                                waits=[g, B["biases"].w], inc=s_vec)
                        PB[bank].r.append(ev)
                        B["h3"].w = ev
                    h3_all_a = (s_act, s_act.n)
                    h3_all_v = (s_vec, s_vec.n)
                    if k + 1 < NT3S:
                        emit_gxload(k + 1)
                    for dh in range(2):
                        for tc in range(4):
                            gidx = dh * 4 + tc
                            bank = 2 + gidx % 4
                            pbw = PB[bank].war()
                            g = None
                            for mcc in range(MC):
                                g = b.op("tensor", lambda e, tc=tc, dh=dh, mcc=mcc, bank=bank: e.matmul(
                                    pb[bank].ap()[:, 0:512],
                                    h3_s.ap()[:, mcc * T3 + tc * P: mcc * T3 + (tc + 1) * P],
                                    w2_s.ap()[:, dh * MC * 512 + mcc * 512:
                                              dh * MC * 512 + (mcc + 1) * 512],
                                    start=(mcc == 0), stop=(mcc == MC - 1)),
                                    waits=([h3_all_a, h3_all_v] + w2_need[dh] + pbw)
                                    if mcc == 0 else [],
                                    inc=(s_pe if mcc == MC - 1 else None))
                            PB[bank].w = g
                            w2_dh_last[dh] = g
                            B["h3"].r.append(g)
                            par = ysb_par
                            ysb_par ^= 1
                            yb = B[f"ysb_{par}"]
                            ev = b.op("vector",
                                      lambda e, bank=bank, dh=dh, par=par:
                                      e.tensor_tensor(
                                          ysb_s[par].ap(), pb[bank].ap()[:, 0:512],
                                          b2b_s.ap()[:, dh * 512:(dh + 1) * 512],
                                          ALU.add),
                                      waits=[g, b2b[dh]] + yb.war(), inc=s_vec)
                            PB[bank].r.append(ev)
                            B["b2r"].r.append(ev)
                            yb.wrote(ev)
                            dsty = ygath_sp[k * T3 + tc * P: k * T3 + (tc + 1) * P,
                                            dh * 512:(dh + 1) * 512]
                            tko = dma("sync", dsty, ysb_s[par].ap(), [ev], d_yo[par])
                            yb.r.append(tko)
                if e_i < NE - 1:
                    dma("gpsimd", b2r_s.ap(), b2_d[e_i + 1:e_i + 2, :],
                        [max(b2_readers, key=lambda t: t[1])], d_wb)
                    for q in range(4):
                        dma("gpsimd", w1_dst(q), w1_src(e_i + 1, q),
                            [w1_q_last[q]], d_w1q[q])
                    for dh in range(2):
                        for half in range(2):
                            dma("gpsimd", w2_dst(dh, half), w2_src(e_i + 1, dh, half),
                                [w2_dh_last[dh]], d_w2q[dh * 2 + half])

            yo_all = [(d_yo[0], d_yo[0].n), (d_yo[1], d_yo[1].n)]
            st3_end = [(s_pe, s_pe.n), (s_act, s_act.n), (s_vec, s_vec.n)]
            for n in ["yt_0", "yt_1", "x1ft_0", "x1ft_1"]:
                B[n].r = list(st3_end) + [max(
                    list(w1_q_last.values()) + list(w2_dh_last.values()),
                    key=lambda t: t[1])]

            NTF = t_core // T3
            for tau in range(NTF):
                par = tau % 2
                yt = yt_s[par]
                ytb = B[f"yt_{par}"]
                xf = x1ft_s[par]
                xfb = B[f"x1ft_{par}"]
                tz = b.op("vector", lambda e, yt=yt: e.memset(yt.ap(), 0.0),
                          waits=ytb.war(), inc=s_vec)
                ytb.wrote(tz)
                gts = []
                for tc in range(4):
                    c = 4 * tau + tc
                    gt = b.op("gpsimd", lambda e, yt=yt, tc=tc, c=c: e.indirect_dma_start(
                        yt.ap()[:, tc * D:(tc + 1) * D], None,
                        ygath_sp, IndirectOffsetOnAxis(dest_i.ap()[:, c:c + 1], 0),
                        bounds_check=bnd_reg, oob_is_err=False),
                        waits=[tz, t_di] + yo_all, inc=d_yg, inck=16)
                    gts.append(gt)
                ytb.wrote((d_yg, d_yg.n))
                xtk = dma("sync", xf.ap().rearrange("p (tc d) -> p tc d", d=D),
                          x1f_sp[tau * T3:(tau + 1) * T3, :].rearrange(
                              "(tc p) d -> p tc d", p=P),
                          xfb.war() + SP["x1f"].rd(), d_xf[par])
                xfb.wrote(xtk)
                fus = []
                for tc in range(4):
                    c = 4 * tau + tc
                    tf0 = b.op("vector", lambda e, xf=xf, yt=yt, tc=tc, c=c:
                               e.scalar_tensor_tensor(
                                   xf.ap()[:, tc * D:(tc + 1) * D],
                                   yt.ap()[:, tc * D:(tc + 1) * D],
                                   p_tip.ap()[:, c:c + 1],
                                   xf.ap()[:, tc * D:(tc + 1) * D],
                                   ALU.mult, ALU.add),
                               waits=[xtk, ytb.w, B["route"].w], inc=s_vec)
                    xfb.wrote(tf0)
                    ytb.r.append(tf0)
                    fus.append(tf0)
                acc_done = fus[-1]
                fin = []
                ln_lane = {}
                for tc in range(4):
                    c0 = 16 + 8 * tc
                    a_sl = xf.ap()[:, tc * D:(tc + 1) * D]
                    tq = b.op("vector", lambda e, a_sl=a_sl: e.tensor_tensor(
                        sq3_s.ap(), a_sl, a_sl, ALU.mult),
                        waits=[acc_done] + B["sq3"].war(), inc=s_vec)
                    t1r = b.op("vector", lambda e, a_sl=a_sl, c0=c0: e.reduce_sum(
                        lnsm_s.ap()[:, c0:c0 + 1], a_sl, axis=AX.X),
                        waits=[acc_done], inc=s_vec)
                    t2r = b.op("vector", lambda e, c0=c0: e.reduce_sum(
                        lnsm_s.ap()[:, c0 + 1:c0 + 2], sq3_s.ap(), axis=AX.X),
                        waits=[tq], inc=s_vec)
                    B["sq3"].wrote(tq)
                    B["sq3"].r.append(t2r)
                    tmu3 = b.op("vector", lambda e, c0=c0: e.tensor_scalar(
                        lnsm_s.ap()[:, c0 + 2:c0 + 3], lnsm_s.ap()[:, c0:c0 + 1],
                        1.0 / D, None, ALU.mult), waits=[t1r], inc=s_vec)
                    tms3 = b.op("vector", lambda e, c0=c0: e.tensor_scalar(
                        lnsm_s.ap()[:, c0 + 3:c0 + 4], lnsm_s.ap()[:, c0 + 1:c0 + 2],
                        1.0 / D, None, ALU.mult), waits=[t2r], inc=s_vec)
                    tm23 = b.op("vector", lambda e, c0=c0: e.tensor_tensor(
                        lnsm_s.ap()[:, c0 + 4:c0 + 5], lnsm_s.ap()[:, c0 + 2:c0 + 3],
                        lnsm_s.ap()[:, c0 + 2:c0 + 3], ALU.mult), waits=[tmu3], inc=s_vec)
                    tvr3 = b.op("vector", lambda e, c0=c0: e.scalar_tensor_tensor(
                        lnsm_s.ap()[:, c0 + 5:c0 + 6], lnsm_s.ap()[:, c0 + 3:c0 + 4],
                        EPS, lnsm_s.ap()[:, c0 + 4:c0 + 5], ALU.add, ALU.subtract),
                        waits=[tms3, tm23], inc=s_vec)
                    ln_lane[tc] = tvr3
                for tc in range(4):
                    c0 = 16 + 8 * tc
                    tl3 = b.op("scalar", lambda e, c0=c0: e.activation(
                        lnsm_s.ap()[:, c0 + 6:c0 + 7], lnsm_s.ap()[:, c0 + 5:c0 + 6],
                        AF.Ln), waits=[ln_lane[tc]], inc=s_act)
                    tr3 = b.op("scalar", lambda e, c0=c0: e.activation(
                        lnsm_s.ap()[:, c0 + 7:c0 + 8], lnsm_s.ap()[:, c0 + 6:c0 + 7],
                        AF.Exp, scale=-0.5), waits=[tl3], inc=s_act)
                    a_sl = xf.ap()[:, tc * D:(tc + 1) * D]
                    tf = b.op("vector", lambda e, a_sl=a_sl, c0=c0: e.tensor_scalar(
                        a_sl, a_sl, lnsm_s.ap()[:, c0 + 2:c0 + 3],
                        lnsm_s.ap()[:, c0 + 7:c0 + 8], ALU.subtract, ALU.mult),
                        waits=[tr3], inc=s_vec)
                    fin.append(tf)
                B["lnsm"].w = fin[-1]
                B["lnsm"].r = []
                xfb.wrote(fin[-1])
                dsto = out_d[tau * T3:(tau + 1) * T3, :].rearrange(
                    "(tc p) d -> p tc d", p=P)
                tko = dma("sync", dsto,
                          xf.ap().rearrange("p (tc d) -> p tc d", d=D),
                          fin, d_out[par])
                xfb.r.append(tko)

            b.op("sync", lambda e: e.nop(),
                 waits=[(d_out[0], d_out[0].n), (d_out[1], d_out[1].n)])

        with nc.Block() as blk:
            @blk.tensor
            def _(e):
                b.replay("tensor", e)

            @blk.vector
            def _(e):
                b.replay("vector", e)

            @blk.scalar
            def _(e):
                b.replay("scalar", e)

            @blk.gpsimd
            def _(e):
                b.replay("gpsimd", e)

            @blk.sync
            def _(e):
                b.replay("sync", e)

    return nc


def _host_layout(inputs, t_core=4096, n_cores=8):
    x = np.ascontiguousarray(inputs["L0_attn_hidden_states"].reshape(-1, D))

    def pcol(v, nch):
        return np.ascontiguousarray(np.asarray(v, np.float32).reshape(nch, P).T)

    shared = dict(
        winT=np.ascontiguousarray(np.asarray(inputs["L0_attn_in_proj_w"], np.float32).T),
        convw=np.ascontiguousarray(
            np.asarray(inputs["L0_attn_conv_w"], np.float32)
            .reshape(NG, 256, 256, KCV).transpose(0, 3, 2, 1)),
        convb=pcol(inputs["L0_attn_conv_b"], IC),
        lng=pcol(inputs["L0_attn_ln_g"], IC),
        lnb=pcol(inputs["L0_attn_ln_b"], IC),
        fcwT=np.ascontiguousarray(np.asarray(inputs["L0_attn_fc_w"], np.float32).T),
        fcb=pcol(inputs["L0_attn_fc_b"], DC),
        cprojwT=np.ascontiguousarray(np.asarray(inputs["L0_attn_cproj_w"], np.float32).T),
        cprojb=pcol(inputs["L0_attn_cproj_b"], IC),
        outwT=np.ascontiguousarray(np.asarray(inputs["L0_attn_out_w"], np.float32).T),
        wg=np.ascontiguousarray(
            np.asarray(inputs["L0_ffn_w_gate"], np.float32)
            .reshape(DC, P, NE).transpose(1, 0, 2)),
        bg=np.ascontiguousarray(np.asarray(inputs["L0_ffn_b_gate"], np.float32).reshape(1, NE)),
        w1=np.ascontiguousarray(np.asarray(inputs["L0_ffn_W1"], np.float32)),
        b1t=np.ascontiguousarray(
            np.asarray(inputs["L0_ffn_b1"], np.float32)
            .reshape(NE, MC, P).transpose(2, 0, 1)),
        w2=np.ascontiguousarray(np.asarray(inputs["L0_ffn_W2"], np.float32)),
        b2=np.ascontiguousarray(np.asarray(inputs["L0_ffn_b2"], np.float32)),
    )
    in_maps = []
    zero_h = np.zeros((3, D), np.float32)
    for c in range(n_cores):
        lo = c * t_core
        m = dict(shared)
        m["x"] = np.ascontiguousarray(x[lo:lo + t_core])
        if lo % 8192 != 0 and lo >= 3:
            m["xh"] = np.ascontiguousarray(x[lo - 3:lo])
        else:
            m["xh"] = zero_h
        in_maps.append(m)
    return in_maps


def kernel(**inputs):
    t_core = 4096
    in_maps = _host_layout(inputs, t_core=t_core, n_cores=8)
    nc = build_program(t_core=t_core, dbg=DBG)
    res = run_bass_kernel_spmd(nc, in_maps, core_ids=list(range(8)))
    out = np.stack([res.results[c]["out"] for c in range(8)], axis=0)
    return out.reshape(4, 8192, D).astype(np.float32)
